# version 27
# baseline (speedup 1.0000x reference)
import numpy as np
from contextlib import ExitStack

import concourse.bass as bass
import concourse.mybir as mybir

F32 = mybir.dt.float32
BF16 = mybir.dt.bfloat16
AF = mybir.ActivationFunctionType
ALU = mybir.AluOpType
AX = mybir.AxisListType

NEG_BIG = float(np.float32(-1.0e30))
LN_EPS = 1e-5
ZA_EPS = 1e-6

ENGINES = ("sync", "tensor", "vector", "scalar", "gpsimd")


def h_hi_full(sb, c):
    n = c.NCH * c.L
    return sb["h_fm"].ap()[:, 0:n // 2].bitcast(BF16)


def h_lo_full(sb, c):
    n = c.NCH * c.L
    return sb["h_fm"].ap()[:, n // 2:n].bitcast(BF16)


def mm3(e, out, lhsT_hi, lhsT_lo, rhs_hi, rhs_lo, start, stop):
    e.matmul(out, lhsT_hi, rhs_hi, start=start, stop=False)
    e.matmul(out, lhsT_hi, rhs_lo, start=False, stop=False)
    return e.matmul(out, lhsT_lo, rhs_hi, start=False, stop=stop)


def preload_lnexp(nc, e):
    from concourse.hw_specs import get_activation_tables
    names = list(get_activation_tables(nc.m.arch).keys())
    idx = names.index("natural_log_exp_and_others")
    return e.add_instruction(mybir.InstLoadActFuncSet(
        name=nc.get_next_instruction_name(),
        act_func_set_id=idx, ins=[], outs=[]))


class Cfg:
    def __init__(self, seqs=2, L=2048, D=1024, H=16, HD=64, upto=4):
        assert HD == 64 and H % 2 == 0 and D == H * HD
        self.SEQS = seqs
        self.L = L
        self.D = D
        self.H = H
        self.HD = HD
        self.NT = L // 128
        self.NCH = D // 128
        self.AG = min(512, L)
        self.NAG = L // self.AG
        self.FG = min(512, L)
        self.NFG = L // self.FG
        self.QW = min(512, L)
        self.upto = upto
        self.scale_b = float(1.0 / np.sqrt(HD))
        self.no_preload = False
        self.nsl = 4


class Ev:
    __slots__ = ("sem", "idx", "val", "eng", "phase")

    def __init__(self, sem, idx):
        self.sem = sem
        self.idx = idx
        self.val = None
        self.eng = None
        self.phase = None


class Sem:
    def __init__(self, es, nc, name):
        self.h = es.enter_context(nc.semaphore(name))
        self.n = 0
        self.name = name


class SemBank:

    def __init__(self, es, nc):
        self.es = es
        self.nc = nc
        self.d = {}

    def get(self, name):
        if name not in self.d:
            self.d[name] = Sem(self.es, self.nc, name)
        return self.d[name]


class Phase:

    def __init__(self, nc, sems):
        self.nc = nc
        self.sems = sems
        self.ops = {e: [] for e in ENGINES}

    def op(self, eng, fn, waits=(), dma=0, channel=None):
        if dma:
            assert channel is not None, "dma ops need a channel"
            sem = self.sems.get("c_" + channel)
        else:
            sem = self.sems.get("e_" + eng)
        ev = Ev(sem, len(self.ops[eng]))
        ev.eng = eng
        ev.phase = self
        ws = []
        same = -1
        own = self.sems.get("e_" + eng)
        for w in waits:
            if w is None:
                continue
            if getattr(w, "phase", None) is self and getattr(w, "eng", None) == eng:
                same = max(same, w.idx)
                continue
            if w.sem is own and w.val is None:
                continue
            ws.append(w)
        self.ops[eng].append([fn, ws, ev, dma, same])
        return ev

    def emit(self):
        for e in ENGINES:
            for fn, ws, ev, ndma, same in self.ops[e]:
                ev.sem.n += 16 * ndma if ndma else 1
                ev.val = ev.sem.n
        phase = self

        with self.nc.Block() as blk:
            for e in ENGINES:
                if not self.ops[e]:
                    continue

                def body(engine, e=e):
                    waited = {}
                    drained_upto = -1
                    for idx, (fn, ws, ev, ndma, same) in enumerate(phase.ops[e]):
                        for w in ws:
                            assert w.val is not None, "wait on unemitted event"
                            if waited.get(w.sem.name, -1) >= w.val:
                                continue
                            engine.wait_ge(w.sem.h, w.val)
                            waited[w.sem.name] = w.val
                        if same >= drained_upto:
                            engine.drain()
                            drained_upto = idx
                        out = fn(engine)
                        if ndma:
                            insts = out if isinstance(out, (list, tuple)) else [out]
                            assert len(insts) == ndma, \
                                f"dma op declared {ndma} but returned {len(insts)}"
                            for inst in insts:
                                inst.then_inc(ev.sem.h, 16)
                        else:
                            out.then_inc(ev.sem.h, 1)
                getattr(blk, e)(body)


_UNIQ = [0]


def uname(base):
    _UNIQ[0] += 1
    return f"{base}_{_UNIQ[0]}"


def build(cfg: Cfg):
    nc = bass.Bass("TRN2", target_bir_lowering=False)
    c = cfg
    D, L, NCH = c.D, c.L, c.NCH

    dram = dict(
        xin=nc.dram_tensor("xin", [c.SEQS, L, D], F32, kind="ExternalInput").ap(),
        yout=nc.dram_tensor("yout", [c.SEQS, L, D], F32, kind="ExternalOutput").ap(),
        wq_cg=nc.dram_tensor("wq_cg", [NCH, D, 128], F32, kind="ExternalInput").ap(),
        wk_half=nc.dram_tensor("wk_half", [2, D, D // 2], F32, kind="ExternalInput").ap(),
        wv_half=nc.dram_tensor("wv_half", [2, D, D // 2], F32, kind="ExternalInput").ap(),
        wo=nc.dram_tensor("wo", [D, D], F32, kind="ExternalInput").ap(),
        bq_pp=nc.dram_tensor("bq_pp", [128, NCH], F32, kind="ExternalInput").ap(),
        bk_bc=nc.dram_tensor("bk_bc", [128, D], F32, kind="ExternalInput").ap(),
        bv_bc=nc.dram_tensor("bv_bc", [128, D], F32, kind="ExternalInput").ap(),
        bo_bc=nc.dram_tensor("bo_bc", [128, D], F32, kind="ExternalInput").ap(),
        wqkv_p=nc.dram_tensor("wqkv_p", [c.H // 2, 3, D, 128], F32, kind="ExternalInput").ap(),
        woutT=nc.dram_tensor("woutT", [D, D], F32, kind="ExternalInput").ap(),
        wrT=nc.dram_tensor("wrT", [D, D], F32, kind="ExternalInput").ap(),
        wkT2=nc.dram_tensor("wkT2", [D, D], F32, kind="ExternalInput").ap(),
        wvT2=nc.dram_tensor("wvT2", [D, D], F32, kind="ExternalInput").ap(),
        mur_pp=nc.dram_tensor("mur_pp", [128, NCH], F32, kind="ExternalInput").ap(),
        muk_pp=nc.dram_tensor("muk_pp", [128, NCH], F32, kind="ExternalInput").ap(),
        ident=nc.dram_tensor("ident", [128, 128], F32, kind="ExternalInput").ap(),
        maskneg=nc.dram_tensor("maskneg", [128, 128], F32, kind="ExternalInput").ap(),
        masknegT=nc.dram_tensor("masknegT", [128, 128], F32, kind="ExternalInput").ap(),
        wr3=nc.dram_tensor("wr3", [D, 2 * D], BF16).ap(),
        wk3=nc.dram_tensor("wk3", [D, 2 * D], BF16).ap(),
        wv3=nc.dram_tensor("wv3", [D, 2 * D], BF16).ap(),
        wq3=nc.dram_tensor("wq3", [NCH, D, 256], BF16).ap(),
        wkh3=nc.dram_tensor("wkh3", [2, D, D], BF16).ap(),
        wvh3=nc.dram_tensor("wvh3", [2, D, D], BF16).ap(),
        wqkv3=nc.dram_tensor("wqkv3", [c.H // 2, 3, D, 256], BF16).ap(),
        rowsel=nc.dram_tensor("rowsel", [2, 128], F32, kind="ExternalInput").ap(),
    )

    es = ExitStack()
    with es:
        NT = c.NT
        sb = dict(
            x_res=nc.alloc_sbuf_tensor("x_res", [128, NT * D], F32),
            h_fm=nc.alloc_sbuf_tensor("h_fm", [128, NCH * L], F32),
            ident=nc.alloc_sbuf_tensor("ident_sb", [128, 128], F32),
            onesb=nc.alloc_sbuf_tensor("ones_sb", [128, 128], F32),
            maskneg=nc.alloc_sbuf_tensor("maskneg_sb", [128, 128], F32),
            masknegT=nc.alloc_sbuf_tensor("masknegT_sb", [128, 128], F32),
            rowsel=nc.alloc_sbuf_tensor("rowsel_sb", [2, 128], F32),
            bq_sb=nc.alloc_sbuf_tensor("bq_sb", [128, NCH], F32),
            mur_sb=nc.alloc_sbuf_tensor("mur_sb", [128, NCH], F32),
            muk_sb=nc.alloc_sbuf_tensor("muk_sb", [128, NCH], F32),
            stats=nc.alloc_sbuf_tensor("stats", [128, 96], F32),
            zero_b=nc.alloc_sbuf_tensor("zero_b", [128, 1], F32),
            eps_b=nc.alloc_sbuf_tensor("eps_b", [128, 1], F32),
        )
        poolt = nc.alloc_sbuf_tensor("pool", [128, 19540], F32)
        pool = Pool(poolt, 19540)

        sems = SemBank(es, nc)

        ph = Phase(nc, sems)
        pairs = (
            (sb["ident"].ap(), dram["ident"]),
            (sb["maskneg"].ap(), dram["maskneg"]),
            (sb["masknegT"].ap(), dram["masknegT"]),
            (sb["rowsel"].ap(), dram["rowsel"]),
            (sb["bq_sb"].ap(), dram["bq_pp"]),
            (sb["mur_sb"].ap(), dram["mur_pp"]),
            (sb["muk_sb"].ap(), dram["muk_pp"]),
        )
        ev_c = ph.op("sync", lambda e: [e.dma_start(d, s) for d, s in pairs],
                     dma=len(pairs), channel="const")
        ph.op("vector", lambda e: e.memset(sb["onesb"].ap(), 1.0))
        ph.op("vector", lambda e: e.memset(sb["zero_b"].ap(), 0.0))
        ph.op("vector", lambda e: e.memset(sb["eps_b"].ap(), float(LN_EPS)))
        ph.op("vector", lambda e: e.engine_nop(), waits=[ev_c])
        ph.emit()

        emit_weight_split(nc, c, pool, dram, sems)

        for s in range(c.SEQS):
            emit_seq(nc, c, s, pool, dram, sb, sems)

    return nc


def emit_weight_split(nc, c, pool, dram, sems):
    D, NCH = c.D, c.NCH
    D2 = D // 2
    pool.reset()
    NS = 4
    W = D
    fin = [pool.take(W) for _ in range(NS)]
    fout = [pool.take(W) for _ in range(NS)]

    items = []
    for ch in range(NCH):
        for nm, dst in (("wrT", "wr3"), ("wkT2", "wk3"), ("wvT2", "wv3")):
            items.append((dram[nm][ch * 128:(ch + 1) * 128, :],
                          dram[dst][ch * 128:(ch + 1) * 128, :], D, False))
    for cg in range(NCH):
        items.append((dram["wq_cg"][cg].rearrange("(ch p) u -> p ch u", p=128),
                      dram["wq3"][cg].rearrange("(ch p) u -> p ch u", p=128),
                      NCH * 128, True))
    for half in range(2):
        for ch in range(NCH):
            items.append((dram["wk_half"][half, ch * 128:(ch + 1) * 128, :],
                          dram["wkh3"][half, ch * 128:(ch + 1) * 128, :],
                          D2, False))
            items.append((dram["wv_half"][half, ch * 128:(ch + 1) * 128, :],
                          dram["wvh3"][half, ch * 128:(ch + 1) * 128, :],
                          D2, False))
    for p in range(c.H // 2):
        for which in range(3):
            items.append((dram["wqkv_p"][p, which]
                          .rearrange("(ch p) u -> p ch u", p=128),
                          dram["wqkv3"][p, which]
                          .rearrange("(ch p) u -> p ch u", p=128),
                          NCH * 128, True))

    ph = Phase(nc, sems)
    ev_in = [None] * len(items)
    ev_sp = [None] * len(items)
    ev_out = [None] * len(items)
    for i, (src, dst, w, tri) in enumerate(items):
        sl = i % NS

        def f_in(e, src=src, sl=sl, w=w, tri=tri):
            d = fin[sl][:, 0:w]
            if tri:
                d = d.rearrange("p (ch u) -> p ch u", u=128)
            return e.dma_start(d, src)
        ev_in[i] = ph.op("sync", f_in, dma=1, channel=f"wsp{sl}",
                         waits=[ev_sp[i - NS] if i >= NS else None])

        def f_split(e, sl=sl, w=w, tri=tri):
            if tri:
                b3 = (fout[sl][:, 0:w].bitcast(BF16)
                      .rearrange("p (ch u) -> p ch u", u=256))
                hi, lo = b3[:, :, 0:128], b3[:, :, 128:256]
                f3 = fin[sl][:, 0:w].rearrange("p (ch u) -> p ch u", u=128)
                e.tensor_copy(hi, f3)
                e.drain()
                return e.tensor_tensor(lo, f3, hi, op=ALU.subtract)
            hi = fout[sl][:, 0:w // 2].bitcast(BF16)
            lo = fout[sl][:, w // 2:w].bitcast(BF16)
            e.tensor_copy(hi, fin[sl][:, 0:w])
            e.drain()
            return e.tensor_tensor(lo, fin[sl][:, 0:w], hi, op=ALU.subtract)
        ev_sp[i] = ph.op("vector", f_split,
                         waits=[ev_in[i], ev_out[i - NS] if i >= NS else None])

        def f_out(e, dst=dst, sl=sl, w=w, tri=tri):
            allb = fout[sl][:, 0:w].bitcast(BF16)
            if tri:
                allb = allb.rearrange("p (ch u) -> p ch u", u=256)
            return e.dma_start(dst, allb)
        ev_out[i] = ph.op("gpsimd", f_out, dma=1, channel=f"gso{sl}",
                          waits=[ev_sp[i]])
    ph.op("vector", lambda e: e.engine_nop(),
          waits=[ev_out[-1], ev_out[-2], ev_out[-3], ev_out[-4]])
    ph.emit()


class Pool:
    def __init__(self, t, n):
        self.t = t
        self.n = n
        self.off = 0

    def reset(self):
        self.off = 0

    def mark(self):
        return self.off

    def reset_to(self, m):
        self.off = m

    def take(self, n):
        assert self.off + n <= self.n, f"pool overflow {self.off}+{n}>{self.n}"
        ap = self.t.ap()[:, self.off:self.off + n]
        self.off += n
        return ap


def emit_seq(nc, c, s, pool, dram, sb, sems):
    D = c.D
    x_res = sb["x_res"]

    ph = Phase(nc, sems)
    src = dram["xin"][s].rearrange("(j p) d -> p j d", p=128)
    dst = x_res.ap().rearrange("p (j d) -> p j d", d=D)
    ev = ph.op("sync", lambda e: e.dma_start(dst, src), dma=1, channel="io")
    ph.op("vector", lambda e: e.engine_nop(), waits=[ev])
    ph.emit()

    if c.upto >= 1:
        emit_attn_a(nc, c, pool, dram, sb, sems)
    else:
        emit_ln(nc, c, pool, sb, sems)
    if c.upto >= 2:
        emit_ffn(nc, c, pool, dram, sb, sems)
    if c.upto >= 3:
        emit_attn_b(nc, c, pool, dram, sb, sems)
    if c.upto >= 4:
        emit_ffn(nc, c, pool, dram, sb, sems)

    ph = Phase(nc, sems)
    dsto = dram["yout"][s].rearrange("(j p) d -> p j d", p=128)
    srco = x_res.ap().rearrange("p (j d) -> p j d", d=D)
    ev = ph.op("sync", lambda e: e.dma_start(dsto, srco), dma=1, channel="io")
    ph.op("vector", lambda e: e.engine_nop(), waits=[ev])
    ph.emit()


def emit_ln(nc, c, pool, sb, sems, prefetch=(), into=None, consumer=None):
    D, L, NT, NCH = c.D, c.L, c.NT, c.NCH
    x_res, h_fm, ident, stats = sb["x_res"], sb["h_fm"], sb["ident"], sb["stats"]
    DH = D // 2
    NCH2 = NCH // 2

    assert 3 * D <= 3072
    h_tm = pool.t.ap()[:, 0:3 * D]

    st = stats.ap()
    MEAN, VAR, STD, ISTD, NMI = 12, 13, 14, 15, 16

    def stc(j, k):
        return st[:, (j % 4) * 24 + k:(j % 4) * 24 + k + 1]

    def stslot(j, a, b):
        return st[:, (j % 4) * 24 + a:(j % 4) * 24 + b]

    es_own = None
    if into is None:
        es_own = ExitStack()
        pts = [es_own.enter_context(
            nc.psum_tensor(uname(f"ln_pt{i}"), [128, DH], F32)) for i in range(4)]
        ph = Phase(nc, sems)
    else:
        ph, pts = into
    pdep = len(pts) // 2

    pf_evs = [ph.op(eng, fn, waits=ws, dma=nd, channel=chan)
              for (eng, fn, nd, chan, ws) in prefetch]
    ev_nrm = [None] * NT
    ev_tp = [[None, None] for _ in range(NT)]
    ev_ev = [[None, None] for _ in range(NT)]
    ev_std = [None] * NT
    ev_s = [None] * NT
    ev_i_prev = [None]

    def emit_head(j):
        xt = x_res.ap()[:, j * D:(j + 1) * D]

        def f_stats(e, j=j, xt=xt):
            nck = (D + 511) // 512
            for q in range(nck):
                wq = min(512, D - q * 512)
                e.bn_stats(stslot(j, q * 6, q * 6 + 6), xt[:, q * 512:q * 512 + wq])
            e.drain()
            return e.bn_aggr(stslot(j, MEAN, MEAN + 2), stslot(j, 0, 6 * nck))
        ev_s[j] = ph.op("vector", f_stats,
                        waits=[ev_nrm[j - 4] if j >= 4 else None, ev_i_prev[0]])
        ev_ln = ph.op("scalar", lambda e, j=j: e.activation(
            stc(j, STD), stc(j, VAR), AF.Ln, bias=sb["eps_b"].ap()),
            waits=[ev_s[j]])
        ev_std[j] = ph.op("scalar", lambda e, j=j: e.activation(
            stc(j, ISTD), stc(j, STD), AF.Exp, bias=sb["zero_b"].ap(),
            scale=-0.5), waits=[ev_ln])

    def emit_tail(j):
        xt = x_res.ap()[:, j * D:(j + 1) * D]
        ht = h_tm[:, (j % 3) * D:(j % 3) * D + D]

        def f_i(e, j=j):
            return e.tensor_scalar(stc(j, NMI), stc(j, MEAN), stc(j, ISTD),
                                   -1.0, op0=ALU.mult, op1=ALU.mult)
        ev_i = ph.op("gpsimd", f_i, waits=[ev_std[j], ev_s[j]])
        ev_i_prev[0] = ev_i
        ev_nrm[j] = ph.op("gpsimd", lambda e, j=j, ht=ht, xt=xt: e.tensor_scalar(
            ht, xt, stc(j, ISTD), stc(j, NMI), op0=ALU.mult, op1=ALU.add),
            waits=[ev_i,
                   ev_tp[j - 3][0] if j >= 3 else None,
                   ev_tp[j - 3][1] if j >= 3 else None])

        for hf in range(2):
            pi = (j % pdep) * 2 + hf

            def f_tp(e, j=j, ht=ht, hf=hf, pi=pi):
                inst = None
                for cc in range(NCH2):
                    cca = hf * NCH2 + cc
                    inst = e.matmul(pts[pi].ap()[:, cc * 128:cc * 128 + 128],
                                    ht[:, cca * 128:cca * 128 + 128],
                                    ident.ap(), start=True, stop=True)
                return inst
            ev_tp[j][hf] = ph.op("tensor", f_tp,
                                 waits=[ev_nrm[j],
                                        ev_ev[j - pdep][hf] if j >= pdep else None])

            def f_evh(e, j=j, hf=hf, pi=pi):
                src = pts[pi].ap().rearrange("p (cc u) -> p cc u", u=128)
                d3 = (h_hi_full(sb, c).rearrange("p (cc u) -> p cc u", cc=NCH)
                      [:, hf * NCH2:(hf + 1) * NCH2, j * 128:(j + 1) * 128])
                return e.copy(d3, src)
            ev_h = ph.op("scalar", f_evh, waits=[ev_tp[j][hf]])

            def f_evl(e, j=j, hf=hf, pi=pi):
                src = pts[pi].ap().rearrange("p (cc u) -> p cc u", u=128)
                hi3 = (h_hi_full(sb, c).rearrange("p (cc u) -> p cc u", cc=NCH)
                       [:, hf * NCH2:(hf + 1) * NCH2, j * 128:(j + 1) * 128])
                lo3 = (h_lo_full(sb, c).rearrange("p (cc u) -> p cc u", cc=NCH)
                       [:, hf * NCH2:(hf + 1) * NCH2, j * 128:(j + 1) * 128])
                return e.tensor_tensor(lo3, src, hi3, op=ALU.subtract)
            ev_ev[j][hf] = ph.op("vector", f_evl, waits=[ev_h])

    LOOK = 4
    nxt = [0]

    def fire_upto(hi):
        while consumer is not None and nxt[0] <= hi:
            consumer(nxt[0], ev_ev[nxt[0]][1])
            nxt[0] += 1

    for j in range(NT):
        emit_head(j)
        if j >= 1:
            emit_tail(j - 1)
            fire_upto(j - 1 - LOOK)
    emit_tail(NT - 1)
    fire_upto(NT - 1)

    tile_done = [ev_ev[j][1] for j in range(NT)]
    if into is None:
        ph.emit()
        es_own.close()
        return pf_evs
    return pf_evs, tile_done


def emit_attn_a(nc, c, pool, dram, sb, sems):
    D, L, H, HD, NT, NCH = c.D, c.L, c.H, c.HD, c.NT, c.NCH
    D2, H2, AG, AGT = D // 2, H // 2, c.AG, c.AG // 128
    SW = max(512, AG)
    x_res, h_fm, onesb, rowsel, bq_sb = (sb["x_res"], sb["h_fm"], sb["onesb"],
                                         sb["rowsel"], sb["bq_sb"])
    VS = H2 * (HD + 1)

    pool.reset_to(3072)
    kv_sb = pool.take(H2 * (HD + 1))
    ksum_cols = pool.take(2 * NCH)
    t_scr = pool.take(2 * max(512, AG))
    e_scr = pool.take(2 * max(512, AG))
    m0 = pool.mark()
    kf_strip = pool.take(2 * D2)
    v_strip = pool.take(2 * VS)
    y_scr = pool.take(2 * 512)
    wk_sb = pool.take(NCH * D2)
    wv_sb = pool.take(NCH * D2)
    bk_sb = pool.take(D)
    bv_sb = pool.take(D)

    def pf_bias(e):
        return [e.dma_start(bk_sb, dram["bk_bc"]),
                e.dma_start(bv_sb, dram["bv_bc"])]

    def wk_b(ch, part):
        b = wk_sb[:, ch * D2:(ch + 1) * D2].bitcast(BF16)
        return b[:, part * D2:(part + 1) * D2]

    def wv_b(ch, part):
        b = wv_sb[:, ch * D2:(ch + 1) * D2].bitcast(BF16)
        return b[:, part * D2:(part + 1) * D2]

    def pf_w0(e):
        insts = []
        for ch in range(NCH):
            insts.append(e.dma_start(
                wk_sb[:, ch * D2:(ch + 1) * D2].bitcast(BF16),
                dram["wkh3"][0, ch * 128:(ch + 1) * 128, :]))
            insts.append(e.dma_start(
                wv_sb[:, ch * D2:(ch + 1) * D2].bitcast(BF16),
                dram["wvh3"][0, ch * 128:(ch + 1) * 128, :]))
        return insts

    with (
        nc.psum_tensor(uname("aa_pk0"), [128, 512], F32) as pk0,
        nc.psum_tensor(uname("aa_pk1"), [128, 512], F32) as pk1,
        nc.psum_tensor(uname("aa_pv0"), [128, 512], F32) as pv0,
        nc.psum_tensor(uname("aa_pv1"), [128, 512], F32) as pv1,
        nc.psum_tensor(uname("aa_pkv0"), [128, 65], F32) as pkv0,
        nc.psum_tensor(uname("aa_pkv1"), [128, 65], F32) as pkv1,
        nc.psum_tensor(uname("aa_ln0"), [128, D // 2], F32) as lpt0,
        nc.psum_tensor(uname("aa_ln1"), [128, D // 2], F32) as lpt1,
    ):
        pks, pvs, pkvs = [pk0, pk1], [pv0, pv1], [pkv0, pkv1]
        ph = Phase(nc, sems)
        ev_b1 = ph.op("sync", pf_bias, dma=2, channel="aab")
        ev_w0 = ph.op("sync", pf_w0, dma=2 * NCH, channel="aaw")
        ev_b2 = ev_b1
        ph.op("vector", lambda e: e.memset(kv_sb, 0.0))
        ph.op("vector", lambda e: e.memset(v_strip, 1.0))
        ph.op("vector", lambda e: e.memset(ksum_cols, 0.0))

        ev_w = [ev_w0, None]
        ev_proj = [None] * (2 * NT)
        ev_strip = [None] * (2 * NT)
        ev_exp = [None] * (2 * NT)
        ev_kvmm = [None] * (2 * NT * H2)
        ev_kvadd = [None] * (2 * NT * H2)

        def emit_kv_tile(i):
            halfi, sl = i // NT, i % 2
            for hh in range(H2):
                habs = halfi * H2 + hh
                ki = i * H2 + hh
                r0 = (habs % 2) * 64

                def f_kv(e, sl=sl, hh=hh, r0=r0):
                    return e.matmul(
                        pkvs[hh % 2].ap()[r0:r0 + 64, 0:65],
                        kf_strip[:, sl * D2 + hh * HD: sl * D2 + (hh + 1) * HD],
                        v_strip[:, sl * VS + hh * (HD + 1): sl * VS + (hh + 1) * (HD + 1)],
                        start=True, stop=True, tile_position=(0, r0))
                ev_kvmm[ki] = ph.op("tensor", f_kv,
                                    waits=[ev_strip[i],
                                           ev_kvadd[ki - 2] if ki >= 2 else None])
                cb = (habs // 2) * 65
                ev_kvadd[ki] = ph.op("vector", lambda e, r0=r0, cb=cb, hh=hh:
                                     e.tensor_tensor(
                                         kv_sb[r0:r0 + 64, cb:cb + 65],
                                         kv_sb[r0:r0 + 64, cb:cb + 65],
                                         pkvs[hh % 2].ap()[r0:r0 + 64, 0:65],
                                         op=ALU.add),
                                     waits=[ev_kvmm[ki],
                                            ev_kvadd[ki - 1] if ki >= 1 else None])
        def consume(half, j, tde):
            i = half * NT + j
            sl = i % 2

            def f_proj(e, half=half, j=j, sl=sl):
                inst = None
                hh = h_hi_full(sb, c)
                hl = h_lo_full(sb, c)
                for ch in range(NCH):
                    co = ch * L + j * 128
                    mm3(e, pks[sl].ap()[:, 0:D2],
                        hh[:, co:co + 128], hl[:, co:co + 128],
                        wk_b(ch, 0), wk_b(ch, 1),
                        start=(ch == 0), stop=(ch == NCH - 1))
                    inst = mm3(e, pvs[sl].ap()[:, 0:D2],
                               hh[:, co:co + 128], hl[:, co:co + 128],
                               wv_b(ch, 0), wv_b(ch, 1),
                               start=(ch == 0), stop=(ch == NCH - 1))
                return inst
            ev_proj[i] = ph.op("tensor", f_proj,
                               waits=[ev_w[half], tde,
                                      ev_strip[i - 2] if i >= 2 else None])
            if i >= 1:
                emit_kv_tile(i - 1)

            def f_ev1(e, half=half, sl=sl):
                yv = y_scr[:, sl * 512:sl * 512 + D2]
                tv = t_scr[:, sl * SW:sl * SW + D2]
                e.tensor_tensor(yv, pks[sl].ap()[:, 0:D2],
                                bk_sb[:, half * D2:half * D2 + D2], op=ALU.add)
                e.drain()
                e.tensor_scalar(tv, yv, 0.0, None, op0=ALU.min)
                vso = v_strip[:, sl * VS:(sl + 1) * VS]
                dstv = vso.rearrange("p (h u) -> p h u", u=HD + 1)[:, :, 0:HD]
                srcv = pvs[sl].ap()[:, 0:D2].rearrange("p (h u) -> p h u", u=HD)
                bias3 = (bv_sb[:, half * D2:half * D2 + D2]
                         .rearrange("p (h u) -> p h u", u=HD))
                return e.tensor_tensor(dstv, srcv, bias3, op=ALU.add)
            ev_e1 = ph.op("vector", f_ev1,
                          waits=[ev_proj[i], ev_b1 if i == 0 else None,
                                 ev_b2 if i == 0 else None,
                                 ev_kvmm[(i - 2) * H2 + H2 - 1] if i >= 2 else None])
            ev_exp[i] = ph.op("scalar", lambda e, sl=sl: e.activation(
                e_scr[:, sl * SW:sl * SW + D2],
                t_scr[:, sl * SW:sl * SW + D2], AF.Exp,
                bias=sb["zero_b"].ap()), waits=[ev_e1])
            ev_strip[i] = ph.op("vector", lambda e, sl=sl: e.scalar_tensor_tensor(
                kf_strip[:, sl * D2:(sl + 1) * D2],
                y_scr[:, sl * 512:sl * 512 + D2], 0.0,
                e_scr[:, sl * SW:sl * SW + D2],
                op0=ALU.max, op1=ALU.add), waits=[ev_exp[i], ev_e1])


        _, tile_done = emit_ln(nc, c, pool, sb, sems, prefetch=[],
                               into=(ph, [lpt0, lpt1]),
                               consumer=lambda j, ev: consume(0, j, ev))

        def f_w(e):
            insts = []
            for ch in range(NCH):
                insts.append(e.dma_start(
                    wk_sb[:, ch * D2:(ch + 1) * D2].bitcast(BF16),
                    dram["wkh3"][1, ch * 128:(ch + 1) * 128, :]))
                insts.append(e.dma_start(
                    wv_sb[:, ch * D2:(ch + 1) * D2].bitcast(BF16),
                    dram["wvh3"][1, ch * 128:(ch + 1) * 128, :]))
            return insts
        ev_w[1] = ph.op("sync", f_w, dma=2 * NCH, channel="aaw",
                        waits=[ev_proj[NT - 1]])
        for j in range(NT):
            consume(1, j, tile_done[j])
        emit_kv_tile(2 * NT - 1)

        def f_ks(e):
            inst = None
            for cg in range(NCH):
                for par in range(2):
                    h = 2 * cg + par
                    r0 = par * 64
                    inst = e.tensor_copy(
                        ksum_cols[r0:r0 + 64, h:h + 1],
                        kv_sb[r0:r0 + 64, (h // 2) * 65 + HD:(h // 2) * 65 + HD + 1])
            return inst
        ph.op("vector", f_ks, waits=[ev_kvadd[-1]])
        ph.emit()

    pool.reset_to(m0)
    qf_g = pool.take(NCH * AG)
    y_g = pool.take(NCH * AG)
    wq_sb = pool.take(2 * NCH * 128)
    wo_sb = pool.take(2 * D)
    zrec_sb = pool.take(AG)
    zbc_sb = pool.take(AG)
    bo_sb = pool.t.ap()[:, 0:D]
    ev_om_prev = [None] * NCH
    ev_qmm_prev = [None] * NCH
    ev_wq01 = [None, None]
    ev_bo = None
    for g in range(c.NAG):
        gc = slice(g * AG, (g + 1) * AG)
        with (
            nc.psum_tensor(uname("ac_pq0"), [128, AG], F32) as pq0,
            nc.psum_tensor(uname("ac_pq1"), [128, AG], F32) as pq1,
            nc.psum_tensor(uname("ac_ps"), [2, AG], F32) as ps_s,
            nc.psum_tensor(uname("ac_pz"), [128, AG], F32) as ps_z,
            nc.psum_tensor(uname("ac_py0"), [128, AG], F32) as py0,
            nc.psum_tensor(uname("ac_py1"), [128, AG], F32) as py1,
        ):
            pqs, pys = [pq0, pq1], [py0, py1]
            ph = Phase(nc, sems)
            if g == 0:
                ev_bo = ph.op("sync", lambda e: e.dma_start(bo_sb, dram["bo_bc"]),
                              dma=1, channel="aab")
            ev_wo01 = []
            for ch in range(2):
                ev_wo01.append(ph.op(
                    "sync", lambda e, ch=ch: e.dma_start(
                        wo_sb[:, ch * D:(ch + 1) * D],
                        dram["wo"][ch * 128:(ch + 1) * 128, :]),
                    dma=1, channel=f"wo{ch}",
                    waits=[ev_om_prev[NCH - 2 + ch]]))
            ev_qf = [None] * NCH
            ev_qmm = [None] * NCH
            ev_wq = [None] * NCH
            ev_wq[0], ev_wq[1] = ev_wq01
            for cg in range(NCH):
                sl = cg % 2

                def f_wq(e, cg=cg, sl=sl):
                    return e.dma_start(
                        wq_sb[:, sl * NCH * 128:(sl + 1) * NCH * 128]
                        .bitcast(BF16).rearrange("p (ch u) -> p ch u", u=256),
                        dram["wq3"][cg].rearrange("(ch p) u -> p ch u", p=128))
                if ev_wq[cg] is None:
                    ev_wq[cg] = ph.op("sync", f_wq, dma=1, channel=f"wq{cg % 2}",
                                      waits=[ev_qmm[cg - 2] if cg >= 2 else None])

                def f_qmm(e, cg=cg, sl=sl):
                    inst = None
                    hh = h_hi_full(sb, c)
                    hl = h_lo_full(sb, c)
                    wb = (wq_sb[:, sl * NCH * 128:(sl + 1) * NCH * 128]
                          .bitcast(BF16))
                    for ch in range(NCH):
                        co = ch * L + g * AG
                        inst = mm3(e, pqs[sl].ap(),
                                   wb[:, ch * 256:ch * 256 + 128],
                                   wb[:, ch * 256 + 128:(ch + 1) * 256],
                                   hh[:, co:co + AG], hl[:, co:co + AG],
                                   start=(ch == 0), stop=(ch == NCH - 1))
                    return inst
                ev_qmm[cg] = ph.op("tensor", f_qmm,
                                   waits=[ev_wq[cg],
                                          ev_qf[cg - 2] if cg >= 2 else None])
                ev_qmm_prev[cg] = ev_qmm[cg]

                def f_t(e, cg=cg, sl=sl):
                    return e.tensor_scalar(t_scr[:, sl * SW:sl * SW + AG],
                                           pqs[sl].ap(), bq_sb[:, cg:cg + 1], 0.0,
                                           op0=ALU.add, op1=ALU.min)
                ev_t = ph.op("vector", f_t, waits=[ev_qmm[cg]])
                ev_e = ph.op("scalar", lambda e, sl=sl: e.activation(
                    e_scr[:, sl * SW:sl * SW + AG],
                    t_scr[:, sl * SW:sl * SW + AG], AF.Exp,
                    bias=sb["zero_b"].ap()), waits=[ev_t])

                def f_qf(e, cg=cg, sl=sl):
                    e.scalar_tensor_tensor(
                        qf_g[:, cg * AG:(cg + 1) * AG], pqs[sl].ap(),
                        bq_sb[:, cg:cg + 1], t_scr[:, sl * SW:sl * SW + AG],
                        op0=ALU.add, op1=ALU.subtract)
                    e.drain()
                    return e.tensor_tensor(
                        qf_g[:, cg * AG:(cg + 1) * AG],
                        qf_g[:, cg * AG:(cg + 1) * AG],
                        e_scr[:, sl * SW:sl * SW + AG], op=ALU.add)
                ev_qf[cg] = ph.op("vector", f_qf, waits=[ev_e])

            ev_ymul = [None] * H
            ev_zc = [None] * H
            ev_smm = [None] * NCH
            ev_z = [None] * NCH
            ev_zbcmm = [None] * H
            ev_ymm = [None] * H
            for cg in range(NCH):
                ev_smm[cg] = ph.op("tensor", lambda e, cg=cg: e.matmul(
                    ps_s.ap(), ksum_cols[:, 2 * cg:2 * cg + 2],
                    qf_g[:, cg * AG:(cg + 1) * AG], start=True, stop=True),
                    waits=[ev_qf[cg],
                           ev_z[cg - 1] if cg >= 1 else None])

                def f_z(e, cg=cg):
                    return e.tensor_scalar(zrec_sb[0:2, :], ps_s.ap(),
                                           float(ZA_EPS), None, op0=ALU.add)
                ev_z[cg] = ph.op("vector", f_z, waits=[ev_smm[cg]])

                ev_zbc = ph.op("tensor", lambda e: e.matmul(
                    ps_z.ap(), rowsel.ap()[0:2, :], zrec_sb[0:2, :],
                    start=True, stop=True),
                    waits=[ev_z[cg],
                           ev_zc[2 * cg - 2] if cg >= 1 else None])

                ev_zcp = ph.op(
                    "scalar", lambda e: e.activation(
                        zbc_sb, ps_z.ap(), AF.Ln, bias=sb["zero_b"].ap()),
                    waits=[ev_zbc,
                           ev_ymul[2 * cg - 1] if cg >= 1 else None])
                ev_zc[2 * cg] = ev_zc[2 * cg + 1] = ph.op(
                    "scalar", lambda e: e.activation(
                        zbc_sb, zbc_sb, AF.Exp, bias=sb["zero_b"].ap(),
                        scale=-1.0), waits=[ev_zcp])
                for par in range(2):
                    h = 2 * cg + par
                    r0 = par * 64
                    ev_ymm[h] = ph.op("tensor", lambda e, r0=r0, cg=cg, par=par: e.matmul(
                        pys[par].ap()[r0:r0 + 64, :],
                        kv_sb[r0:r0 + 64, cg * 65:cg * 65 + HD],
                        qf_g[r0:r0 + 64, cg * AG:(cg + 1) * AG],
                        start=True, stop=True, tile_position=(r0, r0)),
                        waits=[ev_qf[cg],
                               ev_ymul[h - 2] if h >= 2 else None])
                    ev_ymul[h] = ph.op("vector", lambda e, r0=r0, cg=cg, par=par:
                                       e.tensor_tensor(
                                           y_g[r0:r0 + 64, cg * AG:(cg + 1) * AG],
                                           pys[par].ap()[r0:r0 + 64, :],
                                           zbc_sb[r0:r0 + 64, :], op=ALU.mult),
                                       waits=[ev_ymm[h], ev_zc[h]])
            ph.emit()

        with ExitStack() as ps_es:
            ps_o = [ps_es.enter_context(
                nc.psum_tensor(uname(f"ac_po{j}"), [128, D], F32)) for j in range(AGT)]
            ph = Phase(nc, sems)
            ev_wo = [None] * NCH
            ev_wo[0], ev_wo[1] = ev_wo01
            ev_om = [None] * NCH
            for ch in range(NCH):
                sl = ch % 2
                if ch >= 2:
                    ev_wo[ch] = ph.op("sync", lambda e, ch=ch, sl=sl: e.dma_start(
                        wo_sb[:, sl * D:(sl + 1) * D],
                        dram["wo"][ch * 128:(ch + 1) * 128, :]), dma=1,
                        channel=f"wo{ch % 2}",
                        waits=[ev_om[ch - 2]])

                def f_om(e, ch=ch, sl=sl):
                    inst = None
                    HFW = min(512, D)
                    for j in range(AGT):
                        yc = y_g[:, ch * AG + j * 128: ch * AG + (j + 1) * 128]
                        for hf in range(max(1, D // 512)):
                            inst = e.matmul(
                                ps_o[j].ap()[:, hf * HFW:(hf + 1) * HFW], yc,
                                wo_sb[:, sl * D + hf * HFW: sl * D + (hf + 1) * HFW],
                                start=(ch == 0), stop=(ch == NCH - 1))
                    return inst
                ev_om[ch] = ph.op("tensor", f_om, waits=[ev_wo[ch]])
            for j in range(AGT):
                jt = g * AGT + j

                def f_oev(e, j=j, jt=jt):
                    e.tensor_tensor(ps_o[j].ap(), ps_o[j].ap(), bo_sb, op=ALU.add)
                    e.drain()
                    return e.tensor_tensor(
                        x_res.ap()[:, jt * D:(jt + 1) * D],
                        x_res.ap()[:, jt * D:(jt + 1) * D], ps_o[j].ap(), op=ALU.add)
                ph.op("vector", f_oev, waits=[ev_om[NCH - 1], ev_bo])
            ev_om_prev = ev_om
            ev_wq01 = [None, None]
            if g + 1 < c.NAG:
                for cg2 in range(min(2, NCH)):
                    def f_wqp(e, cg2=cg2):
                        return e.dma_start(
                            wq_sb[:, cg2 * NCH * 128:(cg2 + 1) * NCH * 128]
                            .bitcast(BF16).rearrange("p (ch u) -> p ch u", u=256),
                            dram["wq3"][cg2].rearrange("(ch p) u -> p ch u", p=128))
                    ev_wq01[cg2] = ph.op("sync", f_wqp, dma=1, channel=f"wq{cg2}",
                                         waits=[ev_qmm_prev[NCH - 2 + cg2]
                                                if NCH >= 2 else ev_qmm_prev[NCH - 1]])
            ph.emit()


def emit_ffn(nc, c, pool, dram, sb, sems):
    D, L, NT, NCH, FG = c.D, c.L, c.NT, c.NCH, c.FG
    FGT = FG // 128
    x_res, h_fm, mur_sb, muk_sb = sb["x_res"], sb["h_fm"], sb["mur_sb"], sb["muk_sb"]
    HFW = min(512, D)
    NHF = max(1, D // 512)

    pool.reset_to(3072)
    xr_f2 = pool.t.ap()[:, 3072 - 2 * FG:3072]
    xk_f2 = pool.t.ap()[:, 3072 - 4 * FG:3072 - 2 * FG]
    hrec = pool.t.ap()[:, 0:FG + 4]
    xr3 = pool.take(2 * FG)
    xk3 = pool.take(NCH * FG)
    kact3 = xk3
    r_g = pool.take(FGT * D)
    tmp_t = pool.take(max(D, 4 * FG))
    NSL = min(c.nsl, NCH)
    w_sl = pool.take(NSL * D)

    def whi(sl):
        return w_sl[:, sl * D:(sl + 1) * D].bitcast(BF16)[:, 0:D]

    def wlo(sl):
        return w_sl[:, sl * D:(sl + 1) * D].bitcast(BF16)[:, D:2 * D]

    def x3(reg, ch, part, a, b):
        base = reg.bitcast(BF16)
        off = ch * 2 * FG + part * FG
        return base[:, off + a:off + b]

    def wslot_dst(sl):
        return w_sl[:, sl * D:(sl + 1) * D].bitcast(BF16)

    def last_user(c2):
        cands = [ch for ch in range(NCH) if ch % NSL == c2]
        return cands[-1] if cands else None

    pfspecs = [("sync",
                lambda e, c2=c2: e.dma_start(wslot_dst(c2),
                                             dram["wr3"][c2 * 128:(c2 + 1) * 128, :]),
                1, f"ff{c2}", [])
               for c2 in range(NSL)]
    pf = emit_ln(nc, c, pool, sb, sems, prefetch=pfspecs)
    pend = list(pf)

    NPS = max(FGT * NHF, NCH)
    with ExitStack() as ps_es:
        p8 = [ps_es.enter_context(nc.psum_tensor(uname(f"ff_p{i}"), [128, 512], F32))
              for i in range(NPS)]
        ph = Phase(nc, sems)
        ev_rmm_p = [None] * NCH
        ev_omm_p = [None] * NCH
        ev_oev_p = [None] * FGT
        ev_xsp_p = [None] * NCH
        for g in range(c.NFG):
            base = g * FG
            ev_mix = [None] * NCH
            ev_xsp = [None] * NCH
            ev_rmm = [None] * NCH
            ev_sig = [None] * FGT
            ev_kmm = [None] * NCH
            ev_ka = [None] * NCH
            ev_omm = [None] * NCH
            ev_oev = [None] * FGT
            for ch in range(NCH):
                xsl = ch % 2

                xr_f = xr_f2[:, (ch % 2) * FG:(ch % 2 + 1) * FG]
                xk_f = xk_f2[:, (ch % 2) * FG:(ch % 2 + 1) * FG]

                def f_mix(e, ch=ch, base=base, xr_f=xr_f, xk_f=xk_f):
                    hh = h_hi_full(sb, c)
                    hl = h_lo_full(sb, c)
                    if base == 0:
                        hrc = hrec[:, 1:FG + 1]
                        e.tensor_tensor(hrc, hh[:, ch * L: ch * L + FG],
                                        hl[:, ch * L: ch * L + FG], op=ALU.add)
                        e.drain()
                        hc, hp = hrec[:, 2:FG + 1], hrec[:, 1:FG]
                        d = tmp_t[:, 0:FG - 1]
                        e.tensor_sub(d, hc, hp)
                        e.drain()
                        e.scalar_tensor_tensor(xr_f[:, 1:FG], d, mur_sb[:, ch:ch + 1],
                                               hp, op0=ALU.mult, op1=ALU.add)
                        e.scalar_tensor_tensor(xk_f[:, 1:FG], d, muk_sb[:, ch:ch + 1],
                                               hp, op0=ALU.mult, op1=ALU.add)
                        e.tensor_scalar(xr_f[:, 0:1], hrec[:, 1:2], mur_sb[:, ch:ch + 1],
                                        None, op0=ALU.mult)
                        return e.tensor_scalar(xk_f[:, 0:1], hrec[:, 1:2],
                                               muk_sb[:, ch:ch + 1], None, op0=ALU.mult)
                    else:
                        co = ch * L + base - 1
                        e.tensor_tensor(hrec[:, 0:FG + 1], hh[:, co:co + FG + 1],
                                        hl[:, co:co + FG + 1], op=ALU.add)
                        e.drain()
                        hc, hp = hrec[:, 1:FG + 1], hrec[:, 0:FG]
                        d = tmp_t[:, 0:FG]
                        e.tensor_sub(d, hc, hp)
                        e.drain()
                        e.scalar_tensor_tensor(xr_f, d, mur_sb[:, ch:ch + 1], hp,
                                               op0=ALU.mult, op1=ALU.add)
                        return e.scalar_tensor_tensor(xk_f, d, muk_sb[:, ch:ch + 1],
                                                      hp, op0=ALU.mult, op1=ALU.add)
                ev_mix[ch] = ph.op("vector", f_mix,
                                   waits=[ev_xsp[ch - 2] if ch >= 2 else
                                          (ev_xsp_p[NCH - 2 + ch]
                                           if ev_xsp_p[0] else None),
                                          ev_oev_p[-1] if ch == 0 else None])

                def f_xspr(e, ch=ch, xsl=xsl, xr_f=xr_f):
                    e.tensor_copy(xr3.bitcast(BF16)[:, xsl * 2 * FG: xsl * 2 * FG + FG],
                                  xr_f)
                    e.drain()
                    return e.tensor_tensor(
                        xr3.bitcast(BF16)[:, xsl * 2 * FG + FG: (xsl + 1) * 2 * FG],
                        xr_f,
                        xr3.bitcast(BF16)[:, xsl * 2 * FG: xsl * 2 * FG + FG],
                        op=ALU.subtract)
                ev_xr = ph.op("vector", f_xspr,
                              waits=[ev_mix[ch],
                                     ev_rmm[ch - 2] if ch >= 2 else
                                     (ev_rmm_p[NCH - 2 + ch] if ev_rmm_p[0] else None)])

                def f_xspk(e, ch=ch, xk_f=xk_f):
                    e.tensor_copy(xk3.bitcast(BF16)[:, ch * 2 * FG: ch * 2 * FG + FG],
                                  xk_f)
                    e.drain()
                    return e.tensor_tensor(
                        xk3.bitcast(BF16)[:, ch * 2 * FG + FG: (ch + 1) * 2 * FG],
                        xk_f,
                        xk3.bitcast(BF16)[:, ch * 2 * FG: ch * 2 * FG + FG],
                        op=ALU.subtract)
                ev_xsp[ch] = ph.op("gpsimd", f_xspk,
                                   waits=[ev_mix[ch], ev_xr, ev_omm_p[ch]])
                sl = ch % NSL
                if ch >= NSL:
                    pend[sl] = ph.op("sync", lambda e, ch=ch, sl=sl: e.dma_start(
                        wslot_dst(sl),
                        dram["wr3"][ch * 128:(ch + 1) * 128, :]), dma=1,
                        channel=f"ff{sl}",
                        waits=[ev_rmm[ch - NSL]])

                def f_rmm(e, ch=ch, sl=sl, xsl=xsl):
                    inst = None
                    for j in range(FGT):
                        xh = xr3.bitcast(BF16)[:, xsl * 2 * FG + j * 128:
                                               xsl * 2 * FG + (j + 1) * 128]
                        xl = xr3.bitcast(BF16)[:, xsl * 2 * FG + FG + j * 128:
                                               xsl * 2 * FG + FG + (j + 1) * 128]
                        for hf in range(NHF):
                            inst = mm3(e, p8[j * NHF + hf].ap()[:, 0:HFW],
                                       xh, xl,
                                       whi(sl)[:, hf * HFW:(hf + 1) * HFW],
                                       wlo(sl)[:, hf * HFW:(hf + 1) * HFW],
                                       start=(ch == 0), stop=(ch == NCH - 1))
                    return inst
                ev_rmm[ch] = ph.op("tensor", f_rmm,
                                   waits=[pend[sl], ev_xr] +
                                         (ev_oev_p if ch == 0 else []))
            for j in range(FGT):
                def f_sig(e, j=j):
                    inst = None
                    for hf in range(NHF):
                        inst = e.activation(
                            r_g[:, j * D + hf * HFW: j * D + (hf + 1) * HFW],
                            p8[j * NHF + hf].ap()[:, 0:HFW], AF.Sigmoid,
                            bias=sb["zero_b"].ap())
                    return inst
                ev_sig[j] = ph.op("scalar", f_sig,
                                  waits=[ev_rmm[NCH - 1], ev_oev_p[j]])
            for c2 in range(NSL):
                pend[c2] = ph.op("sync", lambda e, c2=c2: e.dma_start(
                    wslot_dst(c2),
                    dram["wk3"][c2 * 128:(c2 + 1) * 128, :]), dma=1,
                    channel=f"ff{c2}",
                    waits=[ev_rmm[last_user(c2)]])
            for ch in range(NCH):
                sl = ch % NSL
                if ch >= NSL:
                    pend[sl] = ph.op("sync", lambda e, ch=ch, sl=sl: e.dma_start(
                        wslot_dst(sl),
                        dram["wk3"][ch * 128:(ch + 1) * 128, :]), dma=1,
                        channel=f"ff{sl}",
                        waits=[ev_kmm[ch - NSL]])

                def f_kmm(e, ch=ch, sl=sl):
                    inst = None
                    xh = xk3.bitcast(BF16)[:, ch * 2 * FG: ch * 2 * FG + FG]
                    xl = xk3.bitcast(BF16)[:, ch * 2 * FG + FG: (ch + 1) * 2 * FG]
                    for cg in range(NCH):
                        inst = mm3(e, p8[cg].ap()[:, 0:FG],
                                   whi(sl)[:, cg * 128:(cg + 1) * 128],
                                   wlo(sl)[:, cg * 128:(cg + 1) * 128],
                                   xh, xl,
                                   start=(ch == 0), stop=(ch == NCH - 1))
                    return inst
                ev_kmm[ch] = ph.op("tensor", f_kmm,
                                   waits=[pend[sl], ev_xsp[ch]] +
                                         (ev_sig[:] if ch == 0 else []))
            ev_ka_prev = {0: None, 1: None}
            ev_kalo = [None] * NCH
            for cg in range(NCH):
                par = cg % 2
                scrA = tmp_t[:, par * FG:(par + 1) * FG]
                scrB = tmp_t[:, (2 + par) * FG:(3 + par) * FG]
                khi = kact3.bitcast(BF16)[:, cg * 2 * FG: cg * 2 * FG + FG]
                klo = kact3.bitcast(BF16)[:, cg * 2 * FG + FG: (cg + 1) * 2 * FG]
                if par == 0:
                    def f_kact(e, cg=cg, scrA=scrA, scrB=scrB, khi=khi):
                        e.activation(scrA, p8[cg].ap()[:, 0:FG], AF.Relu,
                                     bias=sb["zero_b"].ap())
                        e.drain()
                        e.activation(scrB, scrA, AF.Square, bias=sb["zero_b"].ap())
                        e.drain()
                        return e.activation(khi, scrB, AF.Identity,
                                            bias=sb["zero_b"].ap())
                    ev_hi = ph.op("scalar", f_kact,
                                  waits=[ev_kmm[NCH - 1], ev_ka_prev[0]])
                    ev_kalo[cg] = ph.op("vector", lambda e, scrB=scrB, khi=khi, klo=klo:
                                        e.tensor_tensor(klo, scrB, khi,
                                                        op=ALU.subtract),
                                        waits=[ev_hi])
                else:
                    def f_kact(e, cg=cg, scrA=scrA, scrB=scrB, khi=khi, klo=klo):
                        e.tensor_scalar(scrA, p8[cg].ap()[:, 0:FG], 0.0, None,
                                        op0=ALU.max)
                        e.drain()
                        e.tensor_tensor(scrB, scrA, scrA, op=ALU.mult)
                        e.drain()
                        e.tensor_copy(khi, scrB)
                        e.drain()
                        return e.tensor_tensor(klo, scrB, khi, op=ALU.subtract)
                    ev_kalo[cg] = ph.op("vector", f_kact,
                                        waits=[ev_kmm[NCH - 1], ev_ka_prev[1]])
                ev_ka_prev[par] = ev_kalo[cg]
                ev_ka[cg] = ev_kalo[cg]
            for c2 in range(NSL):
                pend[c2] = ph.op("sync", lambda e, c2=c2: e.dma_start(
                    wslot_dst(c2),
                    dram["wv3"][c2 * 128:(c2 + 1) * 128, :]), dma=1,
                    channel=f"ff{c2}",
                    waits=[ev_kmm[last_user(c2)]])
            for ch in range(NCH):
                sl = ch % NSL
                if ch >= NSL:
                    pend[sl] = ph.op("sync", lambda e, ch=ch, sl=sl: e.dma_start(
                        wslot_dst(sl),
                        dram["wv3"][ch * 128:(ch + 1) * 128, :]), dma=1,
                        channel=f"ff{sl}",
                        waits=[ev_omm[ch - NSL]])

                def f_omm(e, ch=ch, sl=sl):
                    inst = None
                    for j in range(FGT):
                        kh = kact3.bitcast(BF16)[:, ch * 2 * FG + j * 128:
                                                 ch * 2 * FG + (j + 1) * 128]
                        kl = kact3.bitcast(BF16)[:, ch * 2 * FG + FG + j * 128:
                                                 ch * 2 * FG + FG + (j + 1) * 128]
                        for hf in range(NHF):
                            inst = mm3(e, p8[j * NHF + hf].ap()[:, 0:HFW],
                                       kh, kl,
                                       whi(sl)[:, hf * HFW:(hf + 1) * HFW],
                                       wlo(sl)[:, hf * HFW:(hf + 1) * HFW],
                                       start=(ch == 0), stop=(ch == NCH - 1))
                    return inst
                ev_omm[ch] = ph.op("tensor", f_omm,
                                   waits=[pend[sl], ev_ka[ch],
                                          ev_ka[NCH - 1] if ch == 0 else None,
                                          (ev_ka[NCH - 2] if NCH >= 2 else None)
                                          if ch == 0 else None])
            for j in range(FGT):
                jt = (base // 128) + j

                def f_oev(e, j=j, jt=jt):
                    for hf in range(NHF):
                        e.tensor_tensor(
                            tmp_t[:, hf * HFW:(hf + 1) * HFW],
                            p8[j * NHF + hf].ap()[:, 0:HFW],
                            r_g[:, j * D + hf * HFW: j * D + (hf + 1) * HFW],
                            op=ALU.mult)
                    e.drain()
                    return e.tensor_tensor(
                        x_res.ap()[:, jt * D:(jt + 1) * D],
                        x_res.ap()[:, jt * D:(jt + 1) * D], tmp_t[:, 0:D],
                        op=ALU.add)
                ev_oev[j] = ph.op("vector", f_oev,
                                  waits=[ev_omm[NCH - 1],
                                         ev_oev[j - 1] if j >= 1 else None])
            if g + 1 < c.NFG:
                for c2 in range(NSL):
                    pend[c2] = ph.op("sync", lambda e, c2=c2: e.dma_start(
                        wslot_dst(c2),
                        dram["wr3"][c2 * 128:(c2 + 1) * 128, :]), dma=1,
                        channel=f"ff{c2}",
                        waits=[ev_omm[last_user(c2)]])
            ev_rmm_p, ev_omm_p, ev_oev_p, ev_xsp_p = ev_rmm, ev_omm, ev_oev, ev_xsp
        ph.emit()


def emit_attn_b(nc, c, pool, dram, sb, sems):
    D, L, H, HD, NT, NCH = c.D, c.L, c.H, c.HD, c.NT, c.NCH
    QW = c.QW
    NQW = L // QW
    QG = min(512, L)
    NQG = L // QG
    GKT = QG // 128
    x_res, h_fm, onesb, masknegT = (sb["x_res"], sb["h_fm"], sb["onesb"],
                                    sb["masknegT"])

    pool.reset_to(3072)
    q_pair = pool.take(L)
    k_pair = pool.take(L)
    v_pair = pool.take(NT * 2 * (HD + 1))
    VT = 2 * (HD + 1)
    pT_sb = pool.take(6 * QG)
    ctx_pair = pool.take(L)
    ctx_b = pool.t.ap()[:, 1024:1024 + L]
    ctx_h = [ctx_pair, ctx_b]
    qw_sb = pool.take(NCH * 128)
    kw_sb = pool.take(NCH * 128)
    vw_sb = pool.take(NCH * 128)
    ow_sb = pool.take(D)
    rden_sb = pool.t.ap()[:, 0:QG]
    rb_sb = pool.t.ap()[:, QG:2 * QG]

    def wslice(buf, ch, part):
        b = buf.bitcast(BF16)
        return b[:, ch * 256 + part * 128: ch * 256 + (part + 1) * 128]

    def f_w_pair(e, p):
        return [
            e.dma_start(qw_sb.bitcast(BF16).rearrange("q (ch u) -> q ch u", u=256),
                        dram["wqkv3"][p, 0].rearrange("(ch q) u -> q ch u", q=128)),
            e.dma_start(kw_sb.bitcast(BF16).rearrange("q (ch u) -> q ch u", u=256),
                        dram["wqkv3"][p, 1].rearrange("(ch q) u -> q ch u", q=128)),
            e.dma_start(vw_sb.bitcast(BF16).rearrange("q (ch u) -> q ch u", u=256),
                        dram["wqkv3"][p, 2].rearrange("(ch q) u -> q ch u", q=128)),
        ]

    ev_w_next = None
    ev_ow_next = None
    ev_projmm_last = None
    ev_vones = None
    tile_done = None
    for p in range(H // 2):
        with ExitStack() as pes:
            pp0 = pes.enter_context(nc.psum_tensor(uname("ab_pp0"), [128, QW], F32))
            pp1 = pes.enter_context(nc.psum_tensor(uname("ab_pp1"), [128, QW], F32))
            pp2 = pes.enter_context(nc.psum_tensor(uname("ab_pp2"), [128, QW], F32))
            pps = [pp0, pp1, pp2]
            ph = Phase(nc, sems)
            if p == 0:
                lpts = [pes.enter_context(nc.psum_tensor(
                    uname(f"ab_ln{i}"), [128, D // 2], F32)) for i in range(4)]
                ev_w_next = ph.op("sync", lambda e: f_w_pair(e, 0),
                                  dma=3, channel="abw")
                ev_ow_next = ph.op("sync", lambda e: e.dma_start(
                    ow_sb, dram["woutT"][0:128, :]), dma=1, channel="abo")
                ev_vones = ph.op("vector", lambda e: e.memset(v_pair, 1.0))
            ev_w = ev_w_next

            ev_ev = [None] * (2 * NQW + NT)
            icnt = [0]
            GQT = QW // 128

            def emit_qk(which, gq, tde):
                i = icnt[0]
                sl = i % 3
                wsb = qw_sb if which == 0 else kw_sb
                dstb = q_pair if which == 0 else k_pair

                def f_mm(e, wsb=wsb, gq=gq, sl=sl):
                    inst = None
                    hh = h_hi_full(sb, c)
                    hl = h_lo_full(sb, c)
                    for ch in range(NCH):
                        co = ch * L + gq * QW
                        inst = mm3(e, pps[sl].ap(),
                                   wslice(wsb, ch, 0), wslice(wsb, ch, 1),
                                   hh[:, co:co + QW], hl[:, co:co + QW],
                                   start=(ch == 0), stop=(ch == NCH - 1))
                    return inst
                ev_mm = ph.op("tensor", f_mm,
                              waits=[ev_w, ev_ev[i - 3] if i >= 3 else None, tde])
                ev_ev[i] = ph.op("scalar", lambda e, dstb=dstb, gq=gq, sl=sl:
                                 e.copy(dstb[:, gq * QW:(gq + 1) * QW],
                                        pps[sl].ap()), waits=[ev_mm])
                icnt[0] += 1

            def emit_v(j, tde):
                i = icnt[0]
                sl = i % 3

                def f_vmm(e, j=j, sl=sl):
                    inst = None
                    hh = h_hi_full(sb, c)
                    hl = h_lo_full(sb, c)
                    for ch in range(NCH):
                        co = ch * L + j * 128
                        inst = mm3(e, pps[sl].ap()[:, 0:128],
                                   hh[:, co:co + 128], hl[:, co:co + 128],
                                   wslice(vw_sb, ch, 0), wslice(vw_sb, ch, 1),
                                   start=(ch == 0), stop=(ch == NCH - 1))
                    return inst
                ev_mm = ph.op("tensor", f_vmm,
                              waits=[ev_w, ev_ev[i - 3] if i >= 3 else None, tde])

                def f_vev(e, j=j, sl=sl):
                    dstv = (v_pair[:, j * VT:(j + 1) * VT]
                            .rearrange("p (h u) -> p h u", u=HD + 1)[:, :, 0:HD])
                    srcv = pps[sl].ap()[:, 0:128].rearrange("p (h u) -> p h u", u=HD)
                    return e.tensor_copy(dstv, srcv)
                ev_ev[i] = ph.op("vector", f_vev,
                                 waits=[ev_mm, ev_vones if p == 0 else None])
                icnt[0] += 1

            def consume_tile(j, tde):
                emit_v(j, tde)
                if j % GQT == GQT - 1:
                    emit_qk(0, j // GQT, tde)
                    emit_qk(1, j // GQT, tde)

            if p == 0:
                emit_ln(nc, c, pool, sb, sems, prefetch=[],
                        into=(ph, lpts), consumer=consume_tile)
            else:
                for j in range(NT):
                    consume_tile(j, None)
            ev_projmm_last = ev_ev[icnt[0] - 1]
            ph.emit()

        with (
            nc.psum_tensor(uname("ab_pT0"), [128, QG], F32) as pT0,
            nc.psum_tensor(uname("ab_pT1"), [128, QG], F32) as pT1,
            nc.psum_tensor(uname("ab_pT2"), [128, QG], F32) as pT2,
            nc.psum_tensor(uname("ab_pc0"), [128, QG], F32) as pc0,
            nc.psum_tensor(uname("ab_pc1"), [128, QG], F32) as pc1,
            nc.psum_tensor(uname("ab_prb"), [128, QG], F32) as ps_rb,
            nc.psum_tensor(uname("ab_po0"), [128, 512], F32) as po0,
            nc.psum_tensor(uname("ab_po1"), [128, 512], F32) as po1,
        ):
            pTs = [pT0, pT1, pT2]
            pcs = [pc0, pc1]
            pos = [po0, po1]
            HFW = min(512, D)
            NHF = max(1, D // 512)
            GT = QG // 128
            ph = Phase(nc, sems)
            lastexp = [None] * 3
            lastpv = [[None] * 3, [None] * 3]
            ev_cx = [None, None]
            ev_cx_last = None
            ev_rbc_prev = None
            ev_rbm_prev = None
            ev_ctxmv = None
            ev_oev2 = []
            ev_omm2_last = None
            pend_out = []

            def emit_outproj(g, ev_mv):
                nonlocal ev_omm2_last
                for hf in range(NHF):
                    for j in range(g * GT, (g + 1) * GT):
                        oi = len(ev_oev2)

                        def f_omm(e, j=j, hf=hf, oi=oi):
                            return e.matmul(
                                pos[oi % 2].ap()[:, 0:HFW],
                                ctx_pair[:, j * 128:(j + 1) * 128],
                                ow_sb[:, hf * HFW:(hf + 1) * HFW],
                                start=True, stop=True)
                        ev_omm2 = ph.op("tensor", f_omm,
                                        waits=[ev_mv,
                                               ev_ow_next,
                                               ev_oev2[oi - 2] if oi >= 2 else None])
                        ev_omm2_last = ev_omm2
                        ev_oev2.append(ph.op(
                            "vector", lambda e, j=j, hf=hf, oi=oi: e.tensor_tensor(
                                x_res.ap()[:, j * D + hf * HFW: j * D + (hf + 1) * HFW],
                                x_res.ap()[:, j * D + hf * HFW: j * D + (hf + 1) * HFW],
                                pos[oi % 2].ap()[:, 0:HFW], op=ALU.add),
                            waits=[ev_omm2]))

            for g in range(NQG):
                nki = GKT * (g + 1)
                ev_exp_k = [[None] * nki, [None] * nki]
                ev_ms_k = [[None] * nki, [None] * nki]

                def emit_pv(ki, phase=ph, g=g, nki=nki, ev_exp_k=ev_exp_k,
                            ev_ms_k=ev_ms_k):
                    for hp in range(2):
                        sl = ki % 3

                        def f_pv(e, hp=hp, sl=sl, ki=ki, g=g, nki=nki):
                            return e.matmul(
                                pcs[hp].ap()[0:65, :],
                                v_pair[:, ki * VT + hp * (HD + 1): ki * VT + (hp + 1) * (HD + 1)],
                                pT_sb[:, (2 * hp + sl if False else hp * 3 + sl) * QG:
                                      (hp * 3 + sl + 1) * QG],
                                start=(ki == 0), stop=(ki == nki - 1))
                        ev = phase.op(
                            "tensor", f_pv,
                            waits=[ev_exp_k[hp][ki], ev_ms_k[hp][ki],
                                   ev_cx[hp] if ki == 0 else None])
                        lastpv[hp][sl] = ev

                for ki in range(nki):
                    diag = ki >= GKT * g
                    moff = (ki - GKT * g) * 128 if diag else 0
                    nq = QG - moff
                    for hp in range(2):
                        r0 = hp * 64
                        psl = (2 * ki + hp) % 3
                        sl = ki % 3

                        def f_mmT(e, hp=hp, r0=r0, psl=psl, ki=ki, g=g, moff=moff, nq=nq):
                            return e.matmul(
                                pTs[psl].ap()[:, 0:nq],
                                k_pair[r0:r0 + 64, ki * 128:(ki + 1) * 128],
                                q_pair[r0:r0 + 64, g * QG + moff: (g + 1) * QG],
                                start=True, stop=True, tile_position=(r0, 0))
                        ev_mmT = ph.op("tensor", f_mmT,
                                       waits=[lastexp[psl]])
                        ev_msk = None
                        if diag:
                            ev_msk = ph.op("vector", lambda e, psl=psl:
                                           e.tensor_tensor(
                                               pTs[psl].ap()[:, 0:128],
                                               pTs[psl].ap()[:, 0:128],
                                               masknegT.ap(), op=ALU.add),
                                           waits=[ev_mmT])
                        ev_ms = None
                        if moff:
                            ev_ms = ph.op("vector", lambda e, hp=hp, sl=sl, moff=moff:
                                          e.memset(pT_sb[:, (hp * 3 + sl) * QG:
                                                         (hp * 3 + sl) * QG + moff], 0.0),
                                          waits=[lastpv[hp][sl]])

                        def f_exp(e, hp=hp, psl=psl, sl=sl, moff=moff, nq=nq):
                            return e.activation(
                                pT_sb[:, (hp * 3 + sl) * QG + moff:(hp * 3 + sl + 1) * QG],
                                pTs[psl].ap()[:, 0:nq], AF.Exp,
                                bias=sb["zero_b"].ap(), scale=c.scale_b)
                        ev_exp_k[hp][ki] = ph.op(
                            "scalar", f_exp,
                            waits=[ev_msk if diag else ev_mmT,
                                   lastpv[hp][sl]])
                        lastexp[psl] = ev_exp_k[hp][ki]
                        ev_ms_k[hp][ki] = ev_ms
                    if ki == 1 and pend_out:
                        emit_outproj(*pend_out.pop(0))
                    if ki >= 2:
                        emit_pv(ki - 2)
                for ki in (nki - 2, nki - 1):
                    if ki >= 0:
                        emit_pv(ki)
                for hp in range(2):
                    def f_dcp(e, hp=hp):
                        return e.activation(
                            rden_sb[64:65, :], pcs[hp].ap()[64:65, :], AF.Ln,
                            bias=sb["zero_b"].ap()[64:65, :])
                    ev_dc = ph.op("scalar", f_dcp,
                                  waits=[lastpv[hp][(nki - 1) % 3], ev_rbm_prev])
                    ev_rbm = ph.op("tensor", lambda e: e.matmul(
                        ps_rb.ap()[0:64, :], onesb.ap()[64:65, 0:64],
                        rden_sb[64:65, :], start=True, stop=True),
                        waits=[ev_dc, ev_rbc_prev])
                    ev_rbm_prev = ev_rbm
                    ev_rbc = ph.op("scalar", lambda e: e.activation(
                        rb_sb[0:64, :], ps_rb.ap()[0:64, :], AF.Exp,
                        bias=sb["zero_b"].ap()[0:64, :], scale=-1.0),
                        waits=[ev_rbm, ev_cx_last])
                    ev_rbc_prev = ev_rbc

                    def f_cx(e, hp=hp, g=g):
                        return e.tensor_tensor(
                            ctx_h[hp][0:64, g * QG:(g + 1) * QG],
                            pcs[hp].ap()[0:64, :],
                            rb_sb[0:64, :], op=ALU.mult)
                    ev_cx[hp] = ph.op("vector", f_cx, waits=[ev_rbc])
                    ev_cx_last = ev_cx[hp]
                ev_ctxmv = ph.op("sync",
                                 lambda e, g=g: e.dma_start(
                                     ctx_pair[64:128, g * QG:(g + 1) * QG],
                                     ctx_b[0:64, g * QG:(g + 1) * QG]),
                                 dma=1, channel="ctxmv",
                                 waits=[ev_cx[1], ev_ctxmv])
                pend_out.append((g, ev_ctxmv))
            while pend_out:
                emit_outproj(*pend_out.pop(0))
            if p + 1 < H // 2:
                ev_w_next = ph.op("sync", lambda e, p=p: f_w_pair(e, p + 1),
                                  dma=3, channel="abw",
                                  waits=[ev_projmm_last])
                ev_ow_next = ph.op(
                    "sync", lambda e, p=p: e.dma_start(
                        ow_sb, dram["woutT"][(p + 1) * 128:(p + 2) * 128, :]),
                    dma=1, channel="abo", waits=[ev_omm2_last])
            ph.emit()


F = np.float32


def host_arrays(cfg, w):
    D, NCH, H = cfg.D, cfg.NCH, cfg.H
    D2 = D // 2
    NP = H // 2
    out = {}
    out["wq_cg"] = np.ascontiguousarray(
        np.stack([w["wq"][:, cg * 128:(cg + 1) * 128] for cg in range(NCH)]))
    out["wk_half"] = np.ascontiguousarray(
        np.stack([w["wk"][:, h * D2:(h + 1) * D2] for h in range(2)]))
    out["wv_half"] = np.ascontiguousarray(
        np.stack([w["wv"][:, h * D2:(h + 1) * D2] for h in range(2)]))
    out["wo"] = np.ascontiguousarray(w["wo"])
    out["bq_pp"] = np.ascontiguousarray(w["bq"].reshape(NCH, 128).T)
    out["bk_bc"] = np.ascontiguousarray(np.tile(w["bk"][None, :], (128, 1)))
    out["bv_bc"] = np.ascontiguousarray(np.tile(w["bv"][None, :], (128, 1)))
    out["bo_bc"] = np.ascontiguousarray(np.tile(w["bo"][None, :], (128, 1)))
    wqkvT = w["wqkvT"]
    out["wqkv_p"] = np.ascontiguousarray(np.stack([
        np.stack([wqkvT[:, which * D + p * 128: which * D + (p + 1) * 128]
                  for which in range(3)])
        for p in range(NP)]))
    out["woutT"] = np.ascontiguousarray(w["woutT"])
    out["wrT"] = np.ascontiguousarray(w["wrT"])
    out["wkT2"] = np.ascontiguousarray(w["wkT2"])
    out["wvT2"] = np.ascontiguousarray(w["wvT2"])
    out["mur_pp"] = np.ascontiguousarray(w["mu_r"].reshape(NCH, 128).T)
    out["muk_pp"] = np.ascontiguousarray(w["mu_k"].reshape(NCH, 128).T)
    out["ident"] = np.eye(128, dtype=F)
    mn = np.zeros((128, 128), F)
    iu = np.triu_indices(128, 1)
    mn[iu] = np.float32(-1.0e30)
    out["maskneg"] = mn
    out["masknegT"] = np.ascontiguousarray(mn.T)
    rs = np.zeros((2, 128), F)
    rs[0, 0:64] = 1.0
    rs[1, 64:128] = 1.0
    out["rowsel"] = rs
    for k in out:
        assert out[k].dtype == np.float32, k
    return out


def rand_logical(cfg, seed=0):
    rng = np.random.default_rng(seed)
    D = cfg.D

    def r(*s, scale=1.0):
        return (rng.standard_normal(s) * scale).astype(F)
    w = dict(
        wq=r(D, D, scale=D ** -0.5), wk=r(D, D, scale=D ** -0.5),
        wv=r(D, D, scale=D ** -0.5), wo=r(D, D, scale=D ** -0.5),
        bq=r(D, scale=1e-3), bk=r(D, scale=1e-3), bv=r(D, scale=1e-3),
        bo=r(D, scale=1e-3),
        wqkvT=r(D, 3 * D, scale=(2 * D) ** -0.5),
        woutT=r(D, D, scale=D ** -0.5),
        wrT=r(D, D, scale=D ** -0.5), wkT2=r(D, D, scale=D ** -0.5),
        wvT2=r(D, D, scale=D ** -0.5),
        mu_r=rng.uniform(0, 1, D).astype(F), mu_k=rng.uniform(0, 1, D).astype(F),
    )
    return w


from concourse.bass_utils import run_bass_kernel_spmd

N_CORES = 8
FULL = dict(seqs=2, L=2048, D=1024, H=16, HD=64, upto=4)


def _logical_weights(inputs):
    D = 1024
    FF = np.float32

    def a(k):
        return np.asarray(inputs[k], FF)
    return dict(
        wq=np.ascontiguousarray(a("L0_attn_a_Wq").reshape(D, D)),
        wk=np.ascontiguousarray(a("L0_attn_a_Wk").reshape(D, D)),
        wv=np.ascontiguousarray(a("L0_attn_a_Wv").reshape(D, D)),
        wo=np.ascontiguousarray(a("L0_attn_a_Wo").reshape(D, D)),
        bq=a("L0_attn_a_bq").reshape(-1),
        bk=a("L0_attn_a_bk").reshape(-1),
        bv=a("L0_attn_a_bv").reshape(-1),
        bo=a("L0_attn_a_bo").reshape(-1),
        wqkvT=np.ascontiguousarray(a("L0_attn_b_w_qkv").T),
        woutT=np.ascontiguousarray(a("L0_attn_b_w_out").T),
        wrT=np.ascontiguousarray(a("L0_ffn_W_r").T),
        wkT2=np.ascontiguousarray(a("L0_ffn_W_k").T),
        wvT2=np.ascontiguousarray(a("L0_ffn_W_v").T),
        mu_r=a("L0_ffn_mu_r"), mu_k=a("L0_ffn_mu_k"),
    )


def kernel(**inputs):
    cfg = Cfg(**FULL)
    w = _logical_weights(inputs)
    warrs = host_arrays(cfg, w)
    x = np.ascontiguousarray(np.asarray(inputs["L0_attn_a_x"], np.float32))
    B = x.shape[0]
    per = B // N_CORES

    nc = build(cfg)
    in_maps = []
    for core in range(N_CORES):
        m = dict(warrs)
        m["xin"] = np.ascontiguousarray(x[core * per:(core + 1) * per])
        in_maps.append(m)
    res = run_bass_kernel_spmd(nc, in_maps, core_ids=list(range(N_CORES)))
    out = np.concatenate([np.asarray(r["yout"]) for r in res.results], axis=0)
    return out.astype(np.float32, copy=False)


# version 30
# speedup vs baseline: 1.1448x; 1.1448x over previous
import numpy as np
from contextlib import ExitStack

import concourse.bass as bass
import concourse.mybir as mybir

F32 = mybir.dt.float32
BF16 = mybir.dt.bfloat16
AF = mybir.ActivationFunctionType
ALU = mybir.AluOpType
AX = mybir.AxisListType

NEG_BIG = float(np.float32(-1.0e30))
LN_EPS = 1e-5
ZA_EPS = 1e-6

ENGINES = ("sync", "tensor", "vector", "scalar", "gpsimd")


def h_hi_full(sb, c):
    n = c.NCH * c.L
    return sb["h_fm"].ap()[:, 0:n // 2].bitcast(BF16)


def h_lo_full(sb, c):
    n = c.NCH * c.L
    return sb["h_fm"].ap()[:, n // 2:n].bitcast(BF16)


def mm3(e, out, lhsT_hi, lhsT_lo, rhs_hi, rhs_lo, start, stop):
    e.matmul(out, lhsT_hi, rhs_hi, start=start, stop=False)
    e.matmul(out, lhsT_hi, rhs_lo, start=False, stop=False)
    return e.matmul(out, lhsT_lo, rhs_hi, start=False, stop=stop)


def preload_lnexp(nc, e):
    from concourse.hw_specs import get_activation_tables
    names = list(get_activation_tables(nc.m.arch).keys())
    idx = names.index("natural_log_exp_and_others")
    return e.add_instruction(mybir.InstLoadActFuncSet(
        name=nc.get_next_instruction_name(),
        act_func_set_id=idx, ins=[], outs=[]))


class Cfg:
    def __init__(self, seqs=2, L=2048, D=1024, H=16, HD=64, upto=4):
        assert HD == 64 and H % 2 == 0 and D == H * HD
        self.SEQS = seqs
        self.L = L
        self.D = D
        self.H = H
        self.HD = HD
        self.NT = L // 128
        self.NCH = D // 128
        self.AG = min(512, L)
        self.NAG = L // self.AG
        self.FG = min(512, L)
        self.NFG = L // self.FG
        self.QW = min(512, L)
        self.upto = upto
        self.scale_b = float(1.0 / np.sqrt(HD))
        self.no_preload = False
        self.nsl = 4


class Ev:
    __slots__ = ("sem", "idx", "val", "eng", "phase")

    def __init__(self, sem, idx):
        self.sem = sem
        self.idx = idx
        self.val = None
        self.eng = None
        self.phase = None


class Sem:
    def __init__(self, es, nc, name):
        self.h = es.enter_context(nc.semaphore(name))
        self.n = 0
        self.name = name


class SemBank:

    def __init__(self, es, nc):
        self.es = es
        self.nc = nc
        self.d = {}

    def get(self, name):
        if name not in self.d:
            self.d[name] = Sem(self.es, self.nc, name)
        return self.d[name]


class Phase:

    def __init__(self, nc, sems):
        self.nc = nc
        self.sems = sems
        self.ops = {e: [] for e in ENGINES}

    def promise(self, eng):
        ev = Ev(self.sems.get("e_" + eng), None)
        ev.eng = eng
        ev.phase = self
        return ev

    def op(self, eng, fn, waits=(), dma=0, channel=None, fulfill=None):
        if dma:
            assert channel is not None, "dma ops need a channel"
            sem = self.sems.get("c_" + channel)
        else:
            sem = self.sems.get("e_" + eng)
        if fulfill is not None:
            assert fulfill.eng == eng and fulfill.idx is None and dma == 0
            ev = fulfill
            ev.idx = len(self.ops[eng])
        else:
            ev = Ev(sem, len(self.ops[eng]))
        ev.eng = eng
        ev.phase = self
        ws = []
        same = -1
        own = self.sems.get("e_" + eng)
        for w in waits:
            if w is None:
                continue
            if getattr(w, "phase", None) is self and getattr(w, "eng", None) == eng:
                assert w.idx is not None, "same-engine wait on unemitted promise"
                same = max(same, w.idx)
                continue
            if w.sem is own and w.val is None:
                continue
            ws.append(w)
        self.ops[eng].append([fn, ws, ev, dma, same])
        return ev

    def emit(self):
        for e in ENGINES:
            for fn, ws, ev, ndma, same in self.ops[e]:
                ev.sem.n += 16 * ndma if ndma else 1
                ev.val = ev.sem.n
        phase = self

        with self.nc.Block() as blk:
            for e in ENGINES:
                if not self.ops[e]:
                    continue

                def body(engine, e=e):
                    waited = {}
                    drained_upto = -1
                    for idx, (fn, ws, ev, ndma, same) in enumerate(phase.ops[e]):
                        for w in ws:
                            assert w.val is not None, "wait on unemitted event"
                            if waited.get(w.sem.name, -1) >= w.val:
                                continue
                            engine.wait_ge(w.sem.h, w.val)
                            waited[w.sem.name] = w.val
                        if same >= drained_upto:
                            engine.drain()
                            drained_upto = idx
                        out = fn(engine)
                        if ndma:
                            insts = out if isinstance(out, (list, tuple)) else [out]
                            assert len(insts) == ndma, \
                                f"dma op declared {ndma} but returned {len(insts)}"
                            for inst in insts:
                                inst.then_inc(ev.sem.h, 16)
                        else:
                            out.then_inc(ev.sem.h, 1)
                getattr(blk, e)(body)


_UNIQ = [0]


def uname(base):
    _UNIQ[0] += 1
    return f"{base}_{_UNIQ[0]}"


def build(cfg: Cfg):
    nc = bass.Bass("TRN2", target_bir_lowering=False)
    c = cfg
    D, L, NCH = c.D, c.L, c.NCH

    dram = dict(
        xin=nc.dram_tensor("xin", [c.SEQS, L, D], F32, kind="ExternalInput").ap(),
        yout=nc.dram_tensor("yout", [c.SEQS, L, D], F32, kind="ExternalOutput").ap(),
        wq_cg=nc.dram_tensor("wq_cg", [NCH, D, 128], F32, kind="ExternalInput").ap(),
        wk_half=nc.dram_tensor("wk_half", [2, D, D // 2], F32, kind="ExternalInput").ap(),
        wv_half=nc.dram_tensor("wv_half", [2, D, D // 2], F32, kind="ExternalInput").ap(),
        wo=nc.dram_tensor("wo", [D, D], F32, kind="ExternalInput").ap(),
        bq_pp=nc.dram_tensor("bq_pp", [128, NCH], F32, kind="ExternalInput").ap(),
        bk_bc=nc.dram_tensor("bk_bc", [128, D], F32, kind="ExternalInput").ap(),
        bv_bc=nc.dram_tensor("bv_bc", [128, D], F32, kind="ExternalInput").ap(),
        bo_bc=nc.dram_tensor("bo_bc", [128, D], F32, kind="ExternalInput").ap(),
        wqkv_p=nc.dram_tensor("wqkv_p", [c.H // 2, 3, D, 128], F32, kind="ExternalInput").ap(),
        woutT=nc.dram_tensor("woutT", [D, D], F32, kind="ExternalInput").ap(),
        wrT=nc.dram_tensor("wrT", [D, D], F32, kind="ExternalInput").ap(),
        wkT2=nc.dram_tensor("wkT2", [D, D], F32, kind="ExternalInput").ap(),
        wvT2=nc.dram_tensor("wvT2", [D, D], F32, kind="ExternalInput").ap(),
        mur_pp=nc.dram_tensor("mur_pp", [128, NCH], F32, kind="ExternalInput").ap(),
        muk_pp=nc.dram_tensor("muk_pp", [128, NCH], F32, kind="ExternalInput").ap(),
        ident=nc.dram_tensor("ident", [128, 128], F32, kind="ExternalInput").ap(),
        maskneg=nc.dram_tensor("maskneg", [128, 128], F32, kind="ExternalInput").ap(),
        masknegT=nc.dram_tensor("masknegT", [128, 128], F32, kind="ExternalInput").ap(),
        wr3=nc.dram_tensor("wr3", [D, 2 * D], BF16).ap(),
        wk3=nc.dram_tensor("wk3", [D, 2 * D], BF16).ap(),
        wv3=nc.dram_tensor("wv3", [D, 2 * D], BF16).ap(),
        wq3=nc.dram_tensor("wq3", [NCH, D, 256], BF16).ap(),
        wkh3=nc.dram_tensor("wkh3", [2, D, D], BF16).ap(),
        wvh3=nc.dram_tensor("wvh3", [2, D, D], BF16).ap(),
        wqkv3=nc.dram_tensor("wqkv3", [c.H // 2, 3, D, 256], BF16).ap(),
        rowsel=nc.dram_tensor("rowsel", [2, 128], F32, kind="ExternalInput").ap(),
    )

    es = ExitStack()
    with es:
        NT = c.NT
        sb = dict(
            x_res=nc.alloc_sbuf_tensor("x_res", [128, NT * D], F32),
            h_fm=nc.alloc_sbuf_tensor("h_fm", [128, NCH * L], F32),
            ident=nc.alloc_sbuf_tensor("ident_sb", [128, 128], F32),
            onesb=nc.alloc_sbuf_tensor("ones_sb", [128, 128], F32),
            maskneg=nc.alloc_sbuf_tensor("maskneg_sb", [128, 128], F32),
            masknegT=nc.alloc_sbuf_tensor("masknegT_sb", [128, 128], F32),
            rowsel=nc.alloc_sbuf_tensor("rowsel_sb", [2, 128], F32),
            bq_sb=nc.alloc_sbuf_tensor("bq_sb", [128, NCH], F32),
            mur_sb=nc.alloc_sbuf_tensor("mur_sb", [128, NCH], F32),
            muk_sb=nc.alloc_sbuf_tensor("muk_sb", [128, NCH], F32),
            stats=nc.alloc_sbuf_tensor("stats", [128, 96], F32),
            zero_b=nc.alloc_sbuf_tensor("zero_b", [128, 1], F32),
            eps_b=nc.alloc_sbuf_tensor("eps_b", [128, 1], F32),
        )
        poolt = nc.alloc_sbuf_tensor("pool", [128, 19540], F32)
        pool = Pool(poolt, 19540)

        sems = SemBank(es, nc)

        ph = Phase(nc, sems)
        pairs = (
            (sb["ident"].ap(), dram["ident"]),
            (sb["maskneg"].ap(), dram["maskneg"]),
            (sb["masknegT"].ap(), dram["masknegT"]),
            (sb["rowsel"].ap(), dram["rowsel"]),
            (sb["bq_sb"].ap(), dram["bq_pp"]),
            (sb["mur_sb"].ap(), dram["mur_pp"]),
            (sb["muk_sb"].ap(), dram["muk_pp"]),
        )
        ev_c = ph.op("sync", lambda e: [e.dma_start(d, s) for d, s in pairs],
                     dma=len(pairs), channel="const")
        ph.op("vector", lambda e: e.memset(sb["onesb"].ap(), 1.0))
        ph.op("vector", lambda e: e.memset(sb["zero_b"].ap(), 0.0))
        ph.op("vector", lambda e: e.memset(sb["eps_b"].ap(), float(LN_EPS)))
        ph.op("vector", lambda e: e.engine_nop(), waits=[ev_c])
        ph.emit()

        emit_weight_split(nc, c, pool, dram, sems)

        for s in range(c.SEQS):
            emit_seq(nc, c, s, pool, dram, sb, sems)

    return nc


def emit_weight_split(nc, c, pool, dram, sems):
    D, NCH = c.D, c.NCH
    D2 = D // 2
    pool.reset()
    NS = 4
    W = D
    fin = [pool.take(W) for _ in range(NS)]
    fout = [pool.take(W) for _ in range(NS)]

    items = []
    for ch in range(NCH):
        for nm, dst in (("wrT", "wr3"), ("wkT2", "wk3"), ("wvT2", "wv3")):
            items.append((dram[nm][ch * 128:(ch + 1) * 128, :],
                          dram[dst][ch * 128:(ch + 1) * 128, :], D, False))
    for cg in range(NCH):
        items.append((dram["wq_cg"][cg].rearrange("(ch p) u -> p ch u", p=128),
                      dram["wq3"][cg].rearrange("(ch p) u -> p ch u", p=128),
                      NCH * 128, True))
    for half in range(2):
        for ch in range(NCH):
            items.append((dram["wk_half"][half, ch * 128:(ch + 1) * 128, :],
                          dram["wkh3"][half, ch * 128:(ch + 1) * 128, :],
                          D2, False))
            items.append((dram["wv_half"][half, ch * 128:(ch + 1) * 128, :],
                          dram["wvh3"][half, ch * 128:(ch + 1) * 128, :],
                          D2, False))
    for p in range(c.H // 2):
        for which in range(3):
            items.append((dram["wqkv_p"][p, which]
                          .rearrange("(ch p) u -> p ch u", p=128),
                          dram["wqkv3"][p, which]
                          .rearrange("(ch p) u -> p ch u", p=128),
                          NCH * 128, True))

    ph = Phase(nc, sems)
    ev_in = [None] * len(items)
    ev_sp = [None] * len(items)
    ev_out = [None] * len(items)
    for i, (src, dst, w, tri) in enumerate(items):
        sl = i % NS

        def f_in(e, src=src, sl=sl, w=w, tri=tri):
            d = fin[sl][:, 0:w]
            if tri:
                d = d.rearrange("p (ch u) -> p ch u", u=128)
            return e.dma_start(d, src)
        ev_in[i] = ph.op("sync", f_in, dma=1, channel=f"wsp{sl}",
                         waits=[ev_sp[i - NS] if i >= NS else None])

        def f_split(e, sl=sl, w=w, tri=tri):
            if tri:
                b3 = (fout[sl][:, 0:w].bitcast(BF16)
                      .rearrange("p (ch u) -> p ch u", u=256))
                hi, lo = b3[:, :, 0:128], b3[:, :, 128:256]
                f3 = fin[sl][:, 0:w].rearrange("p (ch u) -> p ch u", u=128)
                e.tensor_copy(hi, f3)
                e.drain()
                return e.tensor_tensor(lo, f3, hi, op=ALU.subtract)
            hi = fout[sl][:, 0:w // 2].bitcast(BF16)
            lo = fout[sl][:, w // 2:w].bitcast(BF16)
            e.tensor_copy(hi, fin[sl][:, 0:w])
            e.drain()
            return e.tensor_tensor(lo, fin[sl][:, 0:w], hi, op=ALU.subtract)
        ev_sp[i] = ph.op("vector", f_split,
                         waits=[ev_in[i], ev_out[i - NS] if i >= NS else None])

        def f_out(e, dst=dst, sl=sl, w=w, tri=tri):
            allb = fout[sl][:, 0:w].bitcast(BF16)
            if tri:
                allb = allb.rearrange("p (ch u) -> p ch u", u=256)
            return e.dma_start(dst, allb)
        ev_out[i] = ph.op("gpsimd", f_out, dma=1, channel=f"gso{sl}",
                          waits=[ev_sp[i]])
    ph.op("vector", lambda e: e.engine_nop(),
          waits=[ev_out[-1], ev_out[-2], ev_out[-3], ev_out[-4]])
    ph.emit()


class Pool:
    def __init__(self, t, n):
        self.t = t
        self.n = n
        self.off = 0

    def reset(self):
        self.off = 0

    def mark(self):
        return self.off

    def reset_to(self, m):
        self.off = m

    def take(self, n):
        assert self.off + n <= self.n, f"pool overflow {self.off}+{n}>{self.n}"
        ap = self.t.ap()[:, self.off:self.off + n]
        self.off += n
        return ap


def emit_seq(nc, c, s, pool, dram, sb, sems):
    D = c.D
    x_res = sb["x_res"]

    ph = Phase(nc, sems)
    src = dram["xin"][s].rearrange("(j p) d -> p j d", p=128)
    dst = x_res.ap().rearrange("p (j d) -> p j d", d=D)
    ev = ph.op("sync", lambda e: e.dma_start(dst, src), dma=1, channel="io")
    ph.op("vector", lambda e: e.engine_nop(), waits=[ev])
    ph.emit()

    if c.upto >= 1:
        emit_attn_a(nc, c, pool, dram, sb, sems)
    else:
        emit_ln(nc, c, pool, sb, sems)
    if c.upto >= 2:
        emit_ffn(nc, c, pool, dram, sb, sems)
    if c.upto >= 3:
        emit_attn_b(nc, c, pool, dram, sb, sems)
    if c.upto >= 4:
        emit_ffn(nc, c, pool, dram, sb, sems)

    ph = Phase(nc, sems)
    dsto = dram["yout"][s].rearrange("(j p) d -> p j d", p=128)
    srco = x_res.ap().rearrange("p (j d) -> p j d", d=D)
    ev = ph.op("sync", lambda e: e.dma_start(dsto, srco), dma=1, channel="io")
    ph.op("vector", lambda e: e.engine_nop(), waits=[ev])
    ph.emit()


def emit_ln(nc, c, pool, sb, sems, prefetch=(), into=None, consumer=None):
    D, L, NT, NCH = c.D, c.L, c.NT, c.NCH
    x_res, h_fm, ident, stats = sb["x_res"], sb["h_fm"], sb["ident"], sb["stats"]
    DH = D // 2
    NCH2 = NCH // 2

    assert 3 * D <= 3072
    h_tm = pool.t.ap()[:, 0:3 * D]

    st = stats.ap()
    MEAN, VAR, STD, ISTD, NMI = 12, 13, 14, 15, 16

    def stc(j, k):
        return st[:, (j % 4) * 24 + k:(j % 4) * 24 + k + 1]

    def stslot(j, a, b):
        return st[:, (j % 4) * 24 + a:(j % 4) * 24 + b]

    es_own = None
    if into is None:
        es_own = ExitStack()
        pts = [es_own.enter_context(
            nc.psum_tensor(uname(f"ln_pt{i}"), [128, DH], F32)) for i in range(4)]
        ph = Phase(nc, sems)
    else:
        ph, pts = into
    pdep = len(pts) // 2

    pf_evs = [ph.op(eng, fn, waits=ws, dma=nd, channel=chan)
              for (eng, fn, nd, chan, ws) in prefetch]
    ev_nrm = [None] * NT
    ev_tp = [[None, None] for _ in range(NT)]
    ev_ev = [[None, None] for _ in range(NT)]
    ev_std = [None] * NT
    ev_s = [None] * NT
    ev_i_prev = [None]

    def emit_head(j):
        xt = x_res.ap()[:, j * D:(j + 1) * D]

        def f_stats(e, j=j, xt=xt):
            nck = (D + 511) // 512
            for q in range(nck):
                wq = min(512, D - q * 512)
                e.bn_stats(stslot(j, q * 6, q * 6 + 6), xt[:, q * 512:q * 512 + wq])
            e.drain()
            return e.bn_aggr(stslot(j, MEAN, MEAN + 2), stslot(j, 0, 6 * nck))
        ev_s[j] = ph.op("vector", f_stats,
                        waits=[ev_nrm[j - 4] if j >= 4 else None, ev_i_prev[0]])
        ev_ln = ph.op("scalar", lambda e, j=j: e.activation(
            stc(j, STD), stc(j, VAR), AF.Ln, bias=sb["eps_b"].ap()),
            waits=[ev_s[j]])
        ev_std[j] = ph.op("scalar", lambda e, j=j: e.activation(
            stc(j, ISTD), stc(j, STD), AF.Exp, bias=sb["zero_b"].ap(),
            scale=-0.5), waits=[ev_ln])

    def emit_tail(j):
        xt = x_res.ap()[:, j * D:(j + 1) * D]
        ht = h_tm[:, (j % 3) * D:(j % 3) * D + D]

        def f_i(e, j=j):
            return e.tensor_scalar(stc(j, NMI), stc(j, MEAN), stc(j, ISTD),
                                   -1.0, op0=ALU.mult, op1=ALU.mult)
        ev_i = ph.op("gpsimd", f_i, waits=[ev_std[j], ev_s[j]])
        ev_i_prev[0] = ev_i
        ev_nrm[j] = ph.op("gpsimd", lambda e, j=j, ht=ht, xt=xt: e.tensor_scalar(
            ht, xt, stc(j, ISTD), stc(j, NMI), op0=ALU.mult, op1=ALU.add),
            waits=[ev_i,
                   ev_tp[j - 3][0] if j >= 3 else None,
                   ev_tp[j - 3][1] if j >= 3 else None])

        for hf in range(2):
            pi = (j % pdep) * 2 + hf

            def f_tp(e, j=j, ht=ht, hf=hf, pi=pi):
                inst = None
                for cc in range(NCH2):
                    cca = hf * NCH2 + cc
                    inst = e.matmul(pts[pi].ap()[:, cc * 128:cc * 128 + 128],
                                    ht[:, cca * 128:cca * 128 + 128],
                                    ident.ap(), start=True, stop=True)
                return inst
            ev_tp[j][hf] = ph.op("tensor", f_tp,
                                 waits=[ev_nrm[j],
                                        ev_ev[j - pdep][hf] if j >= pdep else None])

            def f_evh(e, j=j, hf=hf, pi=pi):
                src = pts[pi].ap().rearrange("p (cc u) -> p cc u", u=128)
                d3 = (h_hi_full(sb, c).rearrange("p (cc u) -> p cc u", cc=NCH)
                      [:, hf * NCH2:(hf + 1) * NCH2, j * 128:(j + 1) * 128])
                return e.copy(d3, src)
            ev_h = ph.op("scalar", f_evh, waits=[ev_tp[j][hf]])

            def f_evl(e, j=j, hf=hf, pi=pi):
                src = pts[pi].ap().rearrange("p (cc u) -> p cc u", u=128)
                hi3 = (h_hi_full(sb, c).rearrange("p (cc u) -> p cc u", cc=NCH)
                       [:, hf * NCH2:(hf + 1) * NCH2, j * 128:(j + 1) * 128])
                lo3 = (h_lo_full(sb, c).rearrange("p (cc u) -> p cc u", cc=NCH)
                       [:, hf * NCH2:(hf + 1) * NCH2, j * 128:(j + 1) * 128])
                return e.tensor_tensor(lo3, src, hi3, op=ALU.subtract)
            ev_ev[j][hf] = ph.op("vector", f_evl, waits=[ev_h])

    LOOK = 4
    nxt = [0]

    def fire_upto(hi):
        while consumer is not None and nxt[0] <= hi:
            consumer(nxt[0], ev_ev[nxt[0]][1])
            nxt[0] += 1

    for j in range(NT):
        emit_head(j)
        if j >= 1:
            emit_tail(j - 1)
            fire_upto(j - 1 - LOOK)
    emit_tail(NT - 1)
    fire_upto(NT - 1)

    tile_done = [ev_ev[j][1] for j in range(NT)]
    if into is None:
        ph.emit()
        es_own.close()
        return pf_evs
    return pf_evs, tile_done


def emit_attn_a(nc, c, pool, dram, sb, sems):
    D, L, H, HD, NT, NCH = c.D, c.L, c.H, c.HD, c.NT, c.NCH
    D2, H2, AG, AGT = D // 2, H // 2, c.AG, c.AG // 128
    SW = max(512, AG)
    x_res, h_fm, onesb, rowsel, bq_sb = (sb["x_res"], sb["h_fm"], sb["onesb"],
                                         sb["rowsel"], sb["bq_sb"])
    VS = H2 * (HD + 1)

    pool.reset_to(3072)
    kv_sb = pool.take(H2 * (HD + 1))
    ksum_cols = pool.take(2 * NCH)
    t_scr = pool.take(2 * max(512, AG))
    e_scr = pool.take(2 * max(512, AG))
    m0 = pool.mark()
    kf_strip = pool.take(2 * D2)
    v_strip = pool.take(2 * VS)
    y_scr = pool.take(2 * 512)
    wk_sb = pool.take(NCH * D2)
    wv_sb = pool.take(NCH * D2)
    bk_sb = pool.take(D)
    bv_sb = pool.take(D)

    def pf_bias(e):
        return [e.dma_start(bk_sb, dram["bk_bc"]),
                e.dma_start(bv_sb, dram["bv_bc"])]

    def wk_b(ch, part):
        b = wk_sb[:, ch * D2:(ch + 1) * D2].bitcast(BF16)
        return b[:, part * D2:(part + 1) * D2]

    def wv_b(ch, part):
        b = wv_sb[:, ch * D2:(ch + 1) * D2].bitcast(BF16)
        return b[:, part * D2:(part + 1) * D2]

    def pf_w0(e):
        insts = []
        for ch in range(NCH):
            insts.append(e.dma_start(
                wk_sb[:, ch * D2:(ch + 1) * D2].bitcast(BF16),
                dram["wkh3"][0, ch * 128:(ch + 1) * 128, :]))
            insts.append(e.dma_start(
                wv_sb[:, ch * D2:(ch + 1) * D2].bitcast(BF16),
                dram["wvh3"][0, ch * 128:(ch + 1) * 128, :]))
        return insts

    with (
        nc.psum_tensor(uname("aa_pk0"), [128, 512], F32) as pk0,
        nc.psum_tensor(uname("aa_pk1"), [128, 512], F32) as pk1,
        nc.psum_tensor(uname("aa_pv0"), [128, 512], F32) as pv0,
        nc.psum_tensor(uname("aa_pv1"), [128, 512], F32) as pv1,
        nc.psum_tensor(uname("aa_pkv0"), [128, 65], F32) as pkv0,
        nc.psum_tensor(uname("aa_pkv1"), [128, 65], F32) as pkv1,
        nc.psum_tensor(uname("aa_ln0"), [128, D // 2], F32) as lpt0,
        nc.psum_tensor(uname("aa_ln1"), [128, D // 2], F32) as lpt1,
    ):
        pks, pvs, pkvs = [pk0, pk1], [pv0, pv1], [pkv0, pkv1]
        ph = Phase(nc, sems)
        ev_b1 = ph.op("sync", pf_bias, dma=2, channel="aab")
        ev_w0 = ph.op("sync", pf_w0, dma=2 * NCH, channel="aaw")
        ev_b2 = ev_b1
        ph.op("vector", lambda e: e.memset(kv_sb, 0.0))
        ph.op("vector", lambda e: e.memset(v_strip, 1.0))
        ph.op("vector", lambda e: e.memset(ksum_cols, 0.0))

        ev_w = [ev_w0, None]
        ev_proj = [None] * (2 * NT)
        ev_strip = [None] * (2 * NT)
        ev_exp = [None] * (2 * NT)
        ev_kvmm = [None] * (2 * NT * H2)
        ev_kvadd = [None] * (2 * NT * H2)

        def emit_kv_tile(i):
            halfi, sl = i // NT, i % 2
            for hh in range(H2):
                habs = halfi * H2 + hh
                ki = i * H2 + hh
                r0 = (habs % 2) * 64

                def f_kv(e, sl=sl, hh=hh, r0=r0):
                    return e.matmul(
                        pkvs[hh % 2].ap()[r0:r0 + 64, 0:65],
                        kf_strip[:, sl * D2 + hh * HD: sl * D2 + (hh + 1) * HD],
                        v_strip[:, sl * VS + hh * (HD + 1): sl * VS + (hh + 1) * (HD + 1)],
                        start=True, stop=True, tile_position=(0, r0))
                ev_kvmm[ki] = ph.op("tensor", f_kv,
                                    waits=[ev_strip[i],
                                           ev_kvadd[ki - 2] if ki >= 2 else None])
                cb = (habs // 2) * 65
                ev_kvadd[ki] = ph.op("vector", lambda e, r0=r0, cb=cb, hh=hh:
                                     e.tensor_tensor(
                                         kv_sb[r0:r0 + 64, cb:cb + 65],
                                         kv_sb[r0:r0 + 64, cb:cb + 65],
                                         pkvs[hh % 2].ap()[r0:r0 + 64, 0:65],
                                         op=ALU.add),
                                     waits=[ev_kvmm[ki],
                                            ev_kvadd[ki - 1] if ki >= 1 else None])
        def consume(half, j, tde):
            i = half * NT + j
            sl = i % 2

            def f_proj(e, half=half, j=j, sl=sl):
                inst = None
                hh = h_hi_full(sb, c)
                hl = h_lo_full(sb, c)
                for ch in range(NCH):
                    co = ch * L + j * 128
                    mm3(e, pks[sl].ap()[:, 0:D2],
                        hh[:, co:co + 128], hl[:, co:co + 128],
                        wk_b(ch, 0), wk_b(ch, 1),
                        start=(ch == 0), stop=(ch == NCH - 1))
                    inst = mm3(e, pvs[sl].ap()[:, 0:D2],
                               hh[:, co:co + 128], hl[:, co:co + 128],
                               wv_b(ch, 0), wv_b(ch, 1),
                               start=(ch == 0), stop=(ch == NCH - 1))
                return inst
            ev_proj[i] = ph.op("tensor", f_proj,
                               waits=[ev_w[half], tde,
                                      ev_strip[i - 2] if i >= 2 else None])
            if i >= 1:
                emit_kv_tile(i - 1)

            def f_ev1(e, half=half, sl=sl):
                yv = y_scr[:, sl * 512:sl * 512 + D2]
                tv = t_scr[:, sl * SW:sl * SW + D2]
                e.tensor_tensor(yv, pks[sl].ap()[:, 0:D2],
                                bk_sb[:, half * D2:half * D2 + D2], op=ALU.add)
                e.drain()
                e.tensor_scalar(tv, yv, 0.0, None, op0=ALU.min)
                vso = v_strip[:, sl * VS:(sl + 1) * VS]
                dstv = vso.rearrange("p (h u) -> p h u", u=HD + 1)[:, :, 0:HD]
                srcv = pvs[sl].ap()[:, 0:D2].rearrange("p (h u) -> p h u", u=HD)
                bias3 = (bv_sb[:, half * D2:half * D2 + D2]
                         .rearrange("p (h u) -> p h u", u=HD))
                return e.tensor_tensor(dstv, srcv, bias3, op=ALU.add)
            ev_e1 = ph.op("vector", f_ev1,
                          waits=[ev_proj[i], ev_b1 if i == 0 else None,
                                 ev_b2 if i == 0 else None,
                                 ev_kvmm[(i - 2) * H2 + H2 - 1] if i >= 2 else None])
            ev_exp[i] = ph.op("scalar", lambda e, sl=sl: e.activation(
                e_scr[:, sl * SW:sl * SW + D2],
                t_scr[:, sl * SW:sl * SW + D2], AF.Exp,
                bias=sb["zero_b"].ap()), waits=[ev_e1])
            ev_strip[i] = ph.op("vector", lambda e, sl=sl: e.scalar_tensor_tensor(
                kf_strip[:, sl * D2:(sl + 1) * D2],
                y_scr[:, sl * 512:sl * 512 + D2], 0.0,
                e_scr[:, sl * SW:sl * SW + D2],
                op0=ALU.max, op1=ALU.add), waits=[ev_exp[i], ev_e1])


        _, tile_done = emit_ln(nc, c, pool, sb, sems, prefetch=[],
                               into=(ph, [lpt0, lpt1]),
                               consumer=lambda j, ev: consume(0, j, ev))

        def f_w(e):
            insts = []
            for ch in range(NCH):
                insts.append(e.dma_start(
                    wk_sb[:, ch * D2:(ch + 1) * D2].bitcast(BF16),
                    dram["wkh3"][1, ch * 128:(ch + 1) * 128, :]))
                insts.append(e.dma_start(
                    wv_sb[:, ch * D2:(ch + 1) * D2].bitcast(BF16),
                    dram["wvh3"][1, ch * 128:(ch + 1) * 128, :]))
            return insts
        ev_w[1] = ph.op("sync", f_w, dma=2 * NCH, channel="aaw",
                        waits=[ev_proj[NT - 1]])
        for j in range(NT):
            consume(1, j, tile_done[j])
        emit_kv_tile(2 * NT - 1)

        def f_ks(e):
            inst = None
            for cg in range(NCH):
                for par in range(2):
                    h = 2 * cg + par
                    r0 = par * 64
                    inst = e.tensor_copy(
                        ksum_cols[r0:r0 + 64, h:h + 1],
                        kv_sb[r0:r0 + 64, (h // 2) * 65 + HD:(h // 2) * 65 + HD + 1])
            return inst
        ph.op("vector", f_ks, waits=[ev_kvadd[-1]])
        ph.emit()

    pool.reset_to(m0)
    qf_g = pool.take(NCH * AG)
    y_g = pool.take(NCH * AG)
    wq_sb = pool.take(2 * NCH * 128)
    wo_sb = pool.take(2 * D)
    zrec_sb = pool.take(AG)
    zbc_sb = pool.take(AG)
    bo_sb = pool.t.ap()[:, 0:D]
    ev_om_prev = [None] * NCH
    ev_qmm_prev = [None] * NCH
    ev_wq01 = [None, None]
    ev_bo = None
    for g in range(c.NAG):
        gc = slice(g * AG, (g + 1) * AG)
        with (
            nc.psum_tensor(uname("ac_pq0"), [128, AG], F32) as pq0,
            nc.psum_tensor(uname("ac_pq1"), [128, AG], F32) as pq1,
            nc.psum_tensor(uname("ac_ps"), [2, AG], F32) as ps_s,
            nc.psum_tensor(uname("ac_pz"), [128, AG], F32) as ps_z,
            nc.psum_tensor(uname("ac_py0"), [128, AG], F32) as py0,
            nc.psum_tensor(uname("ac_py1"), [128, AG], F32) as py1,
        ):
            pqs, pys = [pq0, pq1], [py0, py1]
            ph = Phase(nc, sems)
            if g == 0:
                ev_bo = ph.op("sync", lambda e: e.dma_start(bo_sb, dram["bo_bc"]),
                              dma=1, channel="aab")
            ev_wo01 = []
            for ch in range(2):
                ev_wo01.append(ph.op(
                    "sync", lambda e, ch=ch: e.dma_start(
                        wo_sb[:, ch * D:(ch + 1) * D],
                        dram["wo"][ch * 128:(ch + 1) * 128, :]),
                    dma=1, channel=f"wo{ch}",
                    waits=[ev_om_prev[NCH - 2 + ch]]))
            ev_qf = [None] * NCH
            ev_qmm = [None] * NCH
            ev_wq = [None] * NCH
            ev_wq[0], ev_wq[1] = ev_wq01
            for cg in range(NCH):
                sl = cg % 2

                def f_wq(e, cg=cg, sl=sl):
                    return e.dma_start(
                        wq_sb[:, sl * NCH * 128:(sl + 1) * NCH * 128]
                        .bitcast(BF16).rearrange("p (ch u) -> p ch u", u=256),
                        dram["wq3"][cg].rearrange("(ch p) u -> p ch u", p=128))
                if ev_wq[cg] is None:
                    ev_wq[cg] = ph.op("sync", f_wq, dma=1, channel=f"wq{cg % 2}",
                                      waits=[ev_qmm[cg - 2] if cg >= 2 else None])

                def f_qmm(e, cg=cg, sl=sl):
                    inst = None
                    hh = h_hi_full(sb, c)
                    hl = h_lo_full(sb, c)
                    wb = (wq_sb[:, sl * NCH * 128:(sl + 1) * NCH * 128]
                          .bitcast(BF16))
                    for ch in range(NCH):
                        co = ch * L + g * AG
                        inst = mm3(e, pqs[sl].ap(),
                                   wb[:, ch * 256:ch * 256 + 128],
                                   wb[:, ch * 256 + 128:(ch + 1) * 256],
                                   hh[:, co:co + AG], hl[:, co:co + AG],
                                   start=(ch == 0), stop=(ch == NCH - 1))
                    return inst
                ev_qmm[cg] = ph.op("tensor", f_qmm,
                                   waits=[ev_wq[cg],
                                          ev_qf[cg - 2] if cg >= 2 else None])
                ev_qmm_prev[cg] = ev_qmm[cg]

                def f_t(e, cg=cg, sl=sl):
                    return e.tensor_scalar(t_scr[:, sl * SW:sl * SW + AG],
                                           pqs[sl].ap(), bq_sb[:, cg:cg + 1], 0.0,
                                           op0=ALU.add, op1=ALU.min)
                ev_t = ph.op("vector", f_t, waits=[ev_qmm[cg]])
                ev_e = ph.op("scalar", lambda e, sl=sl: e.activation(
                    e_scr[:, sl * SW:sl * SW + AG],
                    t_scr[:, sl * SW:sl * SW + AG], AF.Exp,
                    bias=sb["zero_b"].ap()), waits=[ev_t])

                def f_qf(e, cg=cg, sl=sl):
                    e.scalar_tensor_tensor(
                        qf_g[:, cg * AG:(cg + 1) * AG], pqs[sl].ap(),
                        bq_sb[:, cg:cg + 1], t_scr[:, sl * SW:sl * SW + AG],
                        op0=ALU.add, op1=ALU.subtract)
                    e.drain()
                    return e.tensor_tensor(
                        qf_g[:, cg * AG:(cg + 1) * AG],
                        qf_g[:, cg * AG:(cg + 1) * AG],
                        e_scr[:, sl * SW:sl * SW + AG], op=ALU.add)
                ev_qf[cg] = ph.op("vector", f_qf, waits=[ev_e])

            ev_ymul = [None] * H
            ev_zc = [None] * H
            ev_smm = [None] * NCH
            ev_z = [None] * NCH
            ev_zbcmm = [None] * H
            ev_ymm = [None] * H
            for cg in range(NCH):
                ev_smm[cg] = ph.op("tensor", lambda e, cg=cg: e.matmul(
                    ps_s.ap(), ksum_cols[:, 2 * cg:2 * cg + 2],
                    qf_g[:, cg * AG:(cg + 1) * AG], start=True, stop=True),
                    waits=[ev_qf[cg],
                           ev_z[cg - 1] if cg >= 1 else None])

                def f_z(e, cg=cg):
                    return e.tensor_scalar(zrec_sb[0:2, :], ps_s.ap(),
                                           float(ZA_EPS), None, op0=ALU.add)
                ev_z[cg] = ph.op("vector", f_z, waits=[ev_smm[cg]])

                ev_zbc = ph.op("tensor", lambda e: e.matmul(
                    ps_z.ap(), rowsel.ap()[0:2, :], zrec_sb[0:2, :],
                    start=True, stop=True),
                    waits=[ev_z[cg],
                           ev_zc[2 * cg - 2] if cg >= 1 else None])

                ev_zcp = ph.op(
                    "scalar", lambda e: e.activation(
                        zbc_sb, ps_z.ap(), AF.Ln, bias=sb["zero_b"].ap()),
                    waits=[ev_zbc,
                           ev_ymul[2 * cg - 1] if cg >= 1 else None])
                ev_zc[2 * cg] = ev_zc[2 * cg + 1] = ph.op(
                    "scalar", lambda e: e.activation(
                        zbc_sb, zbc_sb, AF.Exp, bias=sb["zero_b"].ap(),
                        scale=-1.0), waits=[ev_zcp])
                for par in range(2):
                    h = 2 * cg + par
                    r0 = par * 64
                    ev_ymm[h] = ph.op("tensor", lambda e, r0=r0, cg=cg, par=par: e.matmul(
                        pys[par].ap()[r0:r0 + 64, :],
                        kv_sb[r0:r0 + 64, cg * 65:cg * 65 + HD],
                        qf_g[r0:r0 + 64, cg * AG:(cg + 1) * AG],
                        start=True, stop=True, tile_position=(r0, r0)),
                        waits=[ev_qf[cg],
                               ev_ymul[h - 2] if h >= 2 else None])
                    ev_ymul[h] = ph.op("vector", lambda e, r0=r0, cg=cg, par=par:
                                       e.tensor_tensor(
                                           y_g[r0:r0 + 64, cg * AG:(cg + 1) * AG],
                                           pys[par].ap()[r0:r0 + 64, :],
                                           zbc_sb[r0:r0 + 64, :], op=ALU.mult),
                                       waits=[ev_ymm[h], ev_zc[h]])
            ph.emit()

        with ExitStack() as ps_es:
            ps_o = [ps_es.enter_context(
                nc.psum_tensor(uname(f"ac_po{j}"), [128, D], F32)) for j in range(AGT)]
            ph = Phase(nc, sems)
            ev_wo = [None] * NCH
            ev_wo[0], ev_wo[1] = ev_wo01
            ev_om = [None] * NCH
            for ch in range(NCH):
                sl = ch % 2
                if ch >= 2:
                    ev_wo[ch] = ph.op("sync", lambda e, ch=ch, sl=sl: e.dma_start(
                        wo_sb[:, sl * D:(sl + 1) * D],
                        dram["wo"][ch * 128:(ch + 1) * 128, :]), dma=1,
                        channel=f"wo{ch % 2}",
                        waits=[ev_om[ch - 2]])

                def f_om(e, ch=ch, sl=sl):
                    inst = None
                    HFW = min(512, D)
                    for j in range(AGT):
                        yc = y_g[:, ch * AG + j * 128: ch * AG + (j + 1) * 128]
                        for hf in range(max(1, D // 512)):
                            inst = e.matmul(
                                ps_o[j].ap()[:, hf * HFW:(hf + 1) * HFW], yc,
                                wo_sb[:, sl * D + hf * HFW: sl * D + (hf + 1) * HFW],
                                start=(ch == 0), stop=(ch == NCH - 1))
                    return inst
                ev_om[ch] = ph.op("tensor", f_om, waits=[ev_wo[ch]])
            for j in range(AGT):
                jt = g * AGT + j

                def f_oev(e, j=j, jt=jt):
                    e.tensor_tensor(ps_o[j].ap(), ps_o[j].ap(), bo_sb, op=ALU.add)
                    e.drain()
                    return e.tensor_tensor(
                        x_res.ap()[:, jt * D:(jt + 1) * D],
                        x_res.ap()[:, jt * D:(jt + 1) * D], ps_o[j].ap(), op=ALU.add)
                ph.op("vector", f_oev, waits=[ev_om[NCH - 1], ev_bo])
            ev_om_prev = ev_om
            ev_wq01 = [None, None]
            if g + 1 < c.NAG:
                for cg2 in range(min(2, NCH)):
                    def f_wqp(e, cg2=cg2):
                        return e.dma_start(
                            wq_sb[:, cg2 * NCH * 128:(cg2 + 1) * NCH * 128]
                            .bitcast(BF16).rearrange("p (ch u) -> p ch u", u=256),
                            dram["wq3"][cg2].rearrange("(ch p) u -> p ch u", p=128))
                    ev_wq01[cg2] = ph.op("sync", f_wqp, dma=1, channel=f"wq{cg2}",
                                         waits=[ev_qmm_prev[NCH - 2 + cg2]
                                                if NCH >= 2 else ev_qmm_prev[NCH - 1]])
            ph.emit()


def emit_ffn(nc, c, pool, dram, sb, sems):
    D, L, NT, NCH, FG = c.D, c.L, c.NT, c.NCH, c.FG
    FGT = FG // 128
    x_res, h_fm, mur_sb, muk_sb = sb["x_res"], sb["h_fm"], sb["mur_sb"], sb["muk_sb"]
    HFW = min(512, D)
    NHF = max(1, D // 512)

    pool.reset_to(3072)
    xr_f2 = pool.t.ap()[:, 3072 - 2 * FG:3072]
    xk_f2 = pool.t.ap()[:, 3072 - 4 * FG:3072 - 2 * FG]
    hrec = pool.t.ap()[:, 0:FG + 4]
    xr3 = pool.take(2 * FG)
    xk3 = pool.take(NCH * FG)
    kact3 = xk3
    r_g = pool.take(FGT * D)
    tmp_t = pool.take(max(D, 4 * FG))
    NSL = min(c.nsl, NCH)
    w_sl = pool.take(NSL * D)

    def whi(sl):
        return w_sl[:, sl * D:(sl + 1) * D].bitcast(BF16)[:, 0:D]

    def wlo(sl):
        return w_sl[:, sl * D:(sl + 1) * D].bitcast(BF16)[:, D:2 * D]

    def x3(reg, ch, part, a, b):
        base = reg.bitcast(BF16)
        off = ch * 2 * FG + part * FG
        return base[:, off + a:off + b]

    def wslot_dst(sl):
        return w_sl[:, sl * D:(sl + 1) * D].bitcast(BF16)

    def last_user(c2):
        cands = [ch for ch in range(NCH) if ch % NSL == c2]
        return cands[-1] if cands else None

    pfspecs = [("sync",
                lambda e, c2=c2: e.dma_start(wslot_dst(c2),
                                             dram["wr3"][c2 * 128:(c2 + 1) * 128, :]),
                1, f"ff{c2}", [])
               for c2 in range(NSL)]
    pf = emit_ln(nc, c, pool, sb, sems, prefetch=pfspecs)
    pend = list(pf)

    NPS = max(FGT * NHF, NCH)
    with ExitStack() as ps_es:
        p8 = [ps_es.enter_context(nc.psum_tensor(uname(f"ff_p{i}"), [128, 512], F32))
              for i in range(NPS)]
        ph = Phase(nc, sems)
        ev_rmm_p = [None] * NCH
        ev_omm_p = [None] * NCH
        ev_oev_p = [None] * FGT
        ev_omul_p = [None] * FGT
        ev_xsp_p = [None] * NCH
        ev_ka_p = [None] * NCH
        prev_tail = [None]

        def emit_tail(base, ev_ommL, omuls, oadds):
            for j in range(FGT):
                jt = (base // 128) + j

                def f_omul(e, j=j):
                    inst = None
                    for hf in range(NHF):
                        inst = e.tensor_tensor(
                            r_g[:, j * D + hf * HFW: j * D + (hf + 1) * HFW],
                            p8[j * NHF + hf].ap()[:, 0:HFW],
                            r_g[:, j * D + hf * HFW: j * D + (hf + 1) * HFW],
                            op=ALU.mult)
                    return inst
                ph.op("vector", f_omul, waits=[ev_ommL], fulfill=omuls[j])

                def f_oadd(e, j=j, jt=jt):
                    return e.tensor_tensor(
                        x_res.ap()[:, jt * D:(jt + 1) * D],
                        x_res.ap()[:, jt * D:(jt + 1) * D],
                        r_g[:, j * D:(j + 1) * D], op=ALU.add)
                ph.op("gpsimd", f_oadd, waits=[omuls[j]], fulfill=oadds[j])
        for g in range(c.NFG):
            base = g * FG
            ev_mix = [None] * NCH
            ev_xsp = [None] * NCH
            ev_rmm = [None] * NCH
            ev_sig = [None] * FGT
            ev_kmm = [None] * NCH
            ev_ka = [None] * NCH
            ev_omm = [None] * NCH
            ev_oev = [None] * FGT
            ev_omul = [None] * FGT
            for ch in range(NCH):
                xsl = ch % 2

                xr_f = xr_f2[:, (ch % 2) * FG:(ch % 2 + 1) * FG]
                xk_f = xk_f2[:, (ch % 2) * FG:(ch % 2 + 1) * FG]

                def f_mix(e, ch=ch, base=base, xr_f=xr_f, xk_f=xk_f):
                    hh = h_hi_full(sb, c)
                    hl = h_lo_full(sb, c)
                    if base == 0:
                        hrc = hrec[:, 1:FG + 1]
                        e.tensor_tensor(hrc, hh[:, ch * L: ch * L + FG],
                                        hl[:, ch * L: ch * L + FG], op=ALU.add)
                        e.drain()
                        hc, hp = hrec[:, 2:FG + 1], hrec[:, 1:FG]
                        d = tmp_t[:, 0:FG - 1]
                        e.tensor_sub(d, hc, hp)
                        e.drain()
                        e.scalar_tensor_tensor(xr_f[:, 1:FG], d, mur_sb[:, ch:ch + 1],
                                               hp, op0=ALU.mult, op1=ALU.add)
                        e.scalar_tensor_tensor(xk_f[:, 1:FG], d, muk_sb[:, ch:ch + 1],
                                               hp, op0=ALU.mult, op1=ALU.add)
                        e.tensor_scalar(xr_f[:, 0:1], hrec[:, 1:2], mur_sb[:, ch:ch + 1],
                                        None, op0=ALU.mult)
                        return e.tensor_scalar(xk_f[:, 0:1], hrec[:, 1:2],
                                               muk_sb[:, ch:ch + 1], None, op0=ALU.mult)
                    else:
                        co = ch * L + base - 1
                        e.tensor_tensor(hrec[:, 0:FG + 1], hh[:, co:co + FG + 1],
                                        hl[:, co:co + FG + 1], op=ALU.add)
                        e.drain()
                        hc, hp = hrec[:, 1:FG + 1], hrec[:, 0:FG]
                        d = tmp_t[:, 0:FG]
                        e.tensor_sub(d, hc, hp)
                        e.drain()
                        e.scalar_tensor_tensor(xr_f, d, mur_sb[:, ch:ch + 1], hp,
                                               op0=ALU.mult, op1=ALU.add)
                        return e.scalar_tensor_tensor(xk_f, d, muk_sb[:, ch:ch + 1],
                                                      hp, op0=ALU.mult, op1=ALU.add)
                ev_mix[ch] = ph.op("vector", f_mix,
                                   waits=[ev_xsp[ch - 2] if ch >= 2 else
                                          (ev_xsp_p[NCH - 2 + ch]
                                           if ev_xsp_p[0] else None),
                                          ev_ka_p[NCH - 1] if ch == 0 else None,
                                          ev_ka_p[NCH - 2] if ch == 0 else None])

                def f_xspr(e, ch=ch, xsl=xsl, xr_f=xr_f):
                    e.tensor_copy(xr3.bitcast(BF16)[:, xsl * 2 * FG: xsl * 2 * FG + FG],
                                  xr_f)
                    e.drain()
                    return e.tensor_tensor(
                        xr3.bitcast(BF16)[:, xsl * 2 * FG + FG: (xsl + 1) * 2 * FG],
                        xr_f,
                        xr3.bitcast(BF16)[:, xsl * 2 * FG: xsl * 2 * FG + FG],
                        op=ALU.subtract)
                ev_xr = ph.op("vector", f_xspr,
                              waits=[ev_mix[ch],
                                     ev_rmm[ch - 2] if ch >= 2 else
                                     (ev_rmm_p[NCH - 2 + ch] if ev_rmm_p[0] else None)])

                def f_xspk(e, ch=ch, xk_f=xk_f):
                    e.tensor_copy(xk3.bitcast(BF16)[:, ch * 2 * FG: ch * 2 * FG + FG],
                                  xk_f)
                    e.drain()
                    return e.tensor_tensor(
                        xk3.bitcast(BF16)[:, ch * 2 * FG + FG: (ch + 1) * 2 * FG],
                        xk_f,
                        xk3.bitcast(BF16)[:, ch * 2 * FG: ch * 2 * FG + FG],
                        op=ALU.subtract)
                ev_xsp[ch] = ph.op("gpsimd", f_xspk,
                                   waits=[ev_mix[ch], ev_xr, ev_omm_p[ch]])
                sl = ch % NSL
                if ch >= NSL:
                    pend[sl] = ph.op("sync", lambda e, ch=ch, sl=sl: e.dma_start(
                        wslot_dst(sl),
                        dram["wr3"][ch * 128:(ch + 1) * 128, :]), dma=1,
                        channel=f"ff{sl}",
                        waits=[ev_rmm[ch - NSL]])

                def f_rmm(e, ch=ch, sl=sl, xsl=xsl):
                    inst = None
                    for j in range(FGT):
                        xh = xr3.bitcast(BF16)[:, xsl * 2 * FG + j * 128:
                                               xsl * 2 * FG + (j + 1) * 128]
                        xl = xr3.bitcast(BF16)[:, xsl * 2 * FG + FG + j * 128:
                                               xsl * 2 * FG + FG + (j + 1) * 128]
                        for hf in range(NHF):
                            inst = mm3(e, p8[j * NHF + hf].ap()[:, 0:HFW],
                                       xh, xl,
                                       whi(sl)[:, hf * HFW:(hf + 1) * HFW],
                                       wlo(sl)[:, hf * HFW:(hf + 1) * HFW],
                                       start=(ch == 0), stop=(ch == NCH - 1))
                    return inst
                ev_rmm[ch] = ph.op("tensor", f_rmm,
                                   waits=[pend[sl], ev_xr] +
                                         (ev_omul_p if ch == 0 else []))
                if ch == 1 and g >= 1 and prev_tail[0] is not None:
                    emit_tail(*prev_tail[0])
            for j in range(FGT):
                def f_sig(e, j=j):
                    inst = None
                    for hf in range(NHF):
                        inst = e.activation(
                            r_g[:, j * D + hf * HFW: j * D + (hf + 1) * HFW],
                            p8[j * NHF + hf].ap()[:, 0:HFW], AF.Sigmoid,
                            bias=sb["zero_b"].ap())
                    return inst
                ev_sig[j] = ph.op("scalar", f_sig,
                                  waits=[ev_rmm[NCH - 1], ev_oev_p[j]])
            for c2 in range(NSL):
                pend[c2] = ph.op("sync", lambda e, c2=c2: e.dma_start(
                    wslot_dst(c2),
                    dram["wk3"][c2 * 128:(c2 + 1) * 128, :]), dma=1,
                    channel=f"ff{c2}",
                    waits=[ev_rmm[last_user(c2)]])
            for ch in range(NCH):
                sl = ch % NSL
                if ch >= NSL:
                    pend[sl] = ph.op("sync", lambda e, ch=ch, sl=sl: e.dma_start(
                        wslot_dst(sl),
                        dram["wk3"][ch * 128:(ch + 1) * 128, :]), dma=1,
                        channel=f"ff{sl}",
                        waits=[ev_kmm[ch - NSL]])

                def f_kmm(e, ch=ch, sl=sl):
                    inst = None
                    xh = xk3.bitcast(BF16)[:, ch * 2 * FG: ch * 2 * FG + FG]
                    xl = xk3.bitcast(BF16)[:, ch * 2 * FG + FG: (ch + 1) * 2 * FG]
                    for cg in range(NCH):
                        inst = mm3(e, p8[cg].ap()[:, 0:FG],
                                   whi(sl)[:, cg * 128:(cg + 1) * 128],
                                   wlo(sl)[:, cg * 128:(cg + 1) * 128],
                                   xh, xl,
                                   start=(ch == 0), stop=(ch == NCH - 1))
                    return inst
                ev_kmm[ch] = ph.op("tensor", f_kmm,
                                   waits=[pend[sl], ev_xsp[ch]] +
                                         (ev_sig[:] if ch == 0 else []))
            ev_ka_prev = {0: None, 1: None}
            ev_kalo = [None] * NCH
            for cg in range(NCH):
                par = cg % 2
                scrA = tmp_t[:, par * FG:(par + 1) * FG]
                scrB = tmp_t[:, (2 + par) * FG:(3 + par) * FG]
                khi = kact3.bitcast(BF16)[:, cg * 2 * FG: cg * 2 * FG + FG]
                klo = kact3.bitcast(BF16)[:, cg * 2 * FG + FG: (cg + 1) * 2 * FG]
                if par == 0:
                    def f_kact(e, cg=cg, scrA=scrA, scrB=scrB, khi=khi):
                        e.activation(scrA, p8[cg].ap()[:, 0:FG], AF.Relu,
                                     bias=sb["zero_b"].ap())
                        e.drain()
                        e.activation(scrB, scrA, AF.Square, bias=sb["zero_b"].ap())
                        e.drain()
                        return e.activation(khi, scrB, AF.Identity,
                                            bias=sb["zero_b"].ap())
                    ev_hi = ph.op("scalar", f_kact,
                                  waits=[ev_kmm[NCH - 1], ev_ka_prev[0]])
                    ev_kalo[cg] = ph.op("vector", lambda e, scrB=scrB, khi=khi, klo=klo:
                                        e.tensor_tensor(klo, scrB, khi,
                                                        op=ALU.subtract),
                                        waits=[ev_hi])
                else:
                    def f_kact(e, cg=cg, scrA=scrA, scrB=scrB, khi=khi, klo=klo):
                        e.tensor_scalar(scrA, p8[cg].ap()[:, 0:FG], 0.0, None,
                                        op0=ALU.max)
                        e.drain()
                        e.tensor_tensor(scrB, scrA, scrA, op=ALU.mult)
                        e.drain()
                        e.tensor_copy(khi, scrB)
                        e.drain()
                        return e.tensor_tensor(klo, scrB, khi, op=ALU.subtract)
                    ev_kalo[cg] = ph.op("vector", f_kact,
                                        waits=[ev_kmm[NCH - 1], ev_ka_prev[1]])
                ev_ka_prev[par] = ev_kalo[cg]
                ev_ka[cg] = ev_kalo[cg]
            for c2 in range(NSL):
                pend[c2] = ph.op("sync", lambda e, c2=c2: e.dma_start(
                    wslot_dst(c2),
                    dram["wv3"][c2 * 128:(c2 + 1) * 128, :]), dma=1,
                    channel=f"ff{c2}",
                    waits=[ev_kmm[last_user(c2)]])
            for ch in range(NCH):
                sl = ch % NSL
                if ch >= NSL:
                    pend[sl] = ph.op("sync", lambda e, ch=ch, sl=sl: e.dma_start(
                        wslot_dst(sl),
                        dram["wv3"][ch * 128:(ch + 1) * 128, :]), dma=1,
                        channel=f"ff{sl}",
                        waits=[ev_omm[ch - NSL]])

                def f_omm(e, ch=ch, sl=sl):
                    inst = None
                    for j in range(FGT):
                        kh = kact3.bitcast(BF16)[:, ch * 2 * FG + j * 128:
                                                 ch * 2 * FG + (j + 1) * 128]
                        kl = kact3.bitcast(BF16)[:, ch * 2 * FG + FG + j * 128:
                                                 ch * 2 * FG + FG + (j + 1) * 128]
                        for hf in range(NHF):
                            inst = mm3(e, p8[j * NHF + hf].ap()[:, 0:HFW],
                                       kh, kl,
                                       whi(sl)[:, hf * HFW:(hf + 1) * HFW],
                                       wlo(sl)[:, hf * HFW:(hf + 1) * HFW],
                                       start=(ch == 0), stop=(ch == NCH - 1))
                    return inst
                ev_omm[ch] = ph.op("tensor", f_omm,
                                   waits=[pend[sl], ev_ka[ch],
                                          ev_ka[NCH - 1] if ch == 0 else None,
                                          (ev_ka[NCH - 2] if NCH >= 2 else None)
                                          if ch == 0 else None])
            for j in range(FGT):
                ev_omul[j] = ph.promise("vector")
                ev_oev[j] = ph.promise("gpsimd")
            prev_tail[0] = (base, ev_omm[NCH - 1], list(ev_omul), list(ev_oev))
            if g == c.NFG - 1:
                emit_tail(*prev_tail[0])
            if g + 1 < c.NFG:
                for c2 in range(NSL):
                    pend[c2] = ph.op("sync", lambda e, c2=c2: e.dma_start(
                        wslot_dst(c2),
                        dram["wr3"][c2 * 128:(c2 + 1) * 128, :]), dma=1,
                        channel=f"ff{c2}",
                        waits=[ev_omm[last_user(c2)]])
            ev_rmm_p, ev_omm_p, ev_oev_p, ev_xsp_p = ev_rmm, ev_omm, ev_oev, ev_xsp
            ev_omul_p, ev_ka_p = ev_omul, ev_ka
        ph.emit()


def emit_attn_b(nc, c, pool, dram, sb, sems):
    D, L, H, HD, NT, NCH = c.D, c.L, c.H, c.HD, c.NT, c.NCH
    QW = c.QW
    NQW = L // QW
    QG = min(512, L)
    NQG = L // QG
    GKT = QG // 128
    x_res, h_fm, onesb, masknegT = (sb["x_res"], sb["h_fm"], sb["onesb"],
                                    sb["masknegT"])

    pool.reset_to(3072)
    q_pair = pool.take(L)
    k_pair = pool.take(L)
    v_pair = pool.take(NT * 2 * (HD + 1))
    VT = 2 * (HD + 1)
    pT_sb = pool.take(6 * QG)
    ctx_pair = pool.take(L)
    ctx_b = pool.t.ap()[:, 1024:1024 + L]
    ctx_h = [ctx_pair, ctx_b]
    qw_sb = pool.take(NCH * 128)
    kw_sb = pool.take(NCH * 128)
    vw_sb = pool.take(NCH * 128)
    ow_sb = pool.take(D)
    rden_sb = pool.t.ap()[:, 0:QG]
    rb_sb = pool.t.ap()[:, QG:2 * QG]

    def wslice(buf, ch, part):
        b = buf.bitcast(BF16)
        return b[:, ch * 256 + part * 128: ch * 256 + (part + 1) * 128]

    def f_w_pair(e, p):
        return [
            e.dma_start(qw_sb.bitcast(BF16).rearrange("q (ch u) -> q ch u", u=256),
                        dram["wqkv3"][p, 0].rearrange("(ch q) u -> q ch u", q=128)),
            e.dma_start(kw_sb.bitcast(BF16).rearrange("q (ch u) -> q ch u", u=256),
                        dram["wqkv3"][p, 1].rearrange("(ch q) u -> q ch u", q=128)),
            e.dma_start(vw_sb.bitcast(BF16).rearrange("q (ch u) -> q ch u", u=256),
                        dram["wqkv3"][p, 2].rearrange("(ch q) u -> q ch u", q=128)),
        ]

    ev_w_next = None
    ev_ow_next = None
    ev_projmm_last = None
    ev_vones = None
    tile_done = None
    for p in range(H // 2):
        with ExitStack() as pes:
            pp0 = pes.enter_context(nc.psum_tensor(uname("ab_pp0"), [128, QW], F32))
            pp1 = pes.enter_context(nc.psum_tensor(uname("ab_pp1"), [128, QW], F32))
            pp2 = pes.enter_context(nc.psum_tensor(uname("ab_pp2"), [128, QW], F32))
            pps = [pp0, pp1, pp2]
            ph = Phase(nc, sems)
            if p == 0:
                lpts = [pes.enter_context(nc.psum_tensor(
                    uname(f"ab_ln{i}"), [128, D // 2], F32)) for i in range(4)]
                ev_w_next = ph.op("sync", lambda e: f_w_pair(e, 0),
                                  dma=3, channel="abw")
                ev_ow_next = ph.op("sync", lambda e: e.dma_start(
                    ow_sb, dram["woutT"][0:128, :]), dma=1, channel="abo")
                ev_vones = ph.op("vector", lambda e: e.memset(v_pair, 1.0))
            ev_w = ev_w_next

            ev_ev = [None] * (2 * NQW + NT)
            icnt = [0]
            GQT = QW // 128

            def emit_qk(which, gq, tde):
                i = icnt[0]
                sl = i % 3
                wsb = qw_sb if which == 0 else kw_sb
                dstb = q_pair if which == 0 else k_pair

                def f_mm(e, wsb=wsb, gq=gq, sl=sl):
                    inst = None
                    hh = h_hi_full(sb, c)
                    hl = h_lo_full(sb, c)
                    for ch in range(NCH):
                        co = ch * L + gq * QW
                        inst = mm3(e, pps[sl].ap(),
                                   wslice(wsb, ch, 0), wslice(wsb, ch, 1),
                                   hh[:, co:co + QW], hl[:, co:co + QW],
                                   start=(ch == 0), stop=(ch == NCH - 1))
                    return inst
                ev_mm = ph.op("tensor", f_mm,
                              waits=[ev_w, ev_ev[i - 3] if i >= 3 else None, tde])
                ev_ev[i] = ph.op("scalar", lambda e, dstb=dstb, gq=gq, sl=sl:
                                 e.copy(dstb[:, gq * QW:(gq + 1) * QW],
                                        pps[sl].ap()), waits=[ev_mm])
                icnt[0] += 1

            def emit_v(j, tde):
                i = icnt[0]
                sl = i % 3

                def f_vmm(e, j=j, sl=sl):
                    inst = None
                    hh = h_hi_full(sb, c)
                    hl = h_lo_full(sb, c)
                    for ch in range(NCH):
                        co = ch * L + j * 128
                        inst = mm3(e, pps[sl].ap()[:, 0:128],
                                   hh[:, co:co + 128], hl[:, co:co + 128],
                                   wslice(vw_sb, ch, 0), wslice(vw_sb, ch, 1),
                                   start=(ch == 0), stop=(ch == NCH - 1))
                    return inst
                ev_mm = ph.op("tensor", f_vmm,
                              waits=[ev_w, ev_ev[i - 3] if i >= 3 else None, tde])

                def f_vev(e, j=j, sl=sl):
                    dstv = (v_pair[:, j * VT:(j + 1) * VT]
                            .rearrange("p (h u) -> p h u", u=HD + 1)[:, :, 0:HD])
                    srcv = pps[sl].ap()[:, 0:128].rearrange("p (h u) -> p h u", u=HD)
                    return e.tensor_copy(dstv, srcv)
                ev_ev[i] = ph.op("vector", f_vev,
                                 waits=[ev_mm, ev_vones if p == 0 else None])
                icnt[0] += 1

            def consume_tile(j, tde):
                emit_v(j, tde)
                if j % GQT == GQT - 1:
                    emit_qk(0, j // GQT, tde)
                    emit_qk(1, j // GQT, tde)

            if p == 0:
                emit_ln(nc, c, pool, sb, sems, prefetch=[],
                        into=(ph, lpts), consumer=consume_tile)
            else:
                for j in range(NT):
                    consume_tile(j, None)
            ev_projmm_last = ev_ev[icnt[0] - 1]
            ph.emit()

        with (
            nc.psum_tensor(uname("ab_pT0"), [128, QG], F32) as pT0,
            nc.psum_tensor(uname("ab_pT1"), [128, QG], F32) as pT1,
            nc.psum_tensor(uname("ab_pT2"), [128, QG], F32) as pT2,
            nc.psum_tensor(uname("ab_pc0"), [128, QG], F32) as pc0,
            nc.psum_tensor(uname("ab_pc1"), [128, QG], F32) as pc1,
            nc.psum_tensor(uname("ab_prb"), [128, QG], F32) as ps_rb,
            nc.psum_tensor(uname("ab_po0"), [128, 512], F32) as po0,
            nc.psum_tensor(uname("ab_po1"), [128, 512], F32) as po1,
        ):
            pTs = [pT0, pT1, pT2]
            pcs = [pc0, pc1]
            pos = [po0, po1]
            HFW = min(512, D)
            NHF = max(1, D // 512)
            GT = QG // 128
            ph = Phase(nc, sems)
            lastexp = [None] * 3
            lastpv = [[None] * 3, [None] * 3]
            ev_cx = [None, None]
            ev_cx_last = None
            ev_rbc_prev = None
            ev_rbm_prev = None
            ev_ctxmv = None
            ev_oev2 = []
            ev_omm2_last = None
            pend_out = []

            def emit_outproj(g, ev_mv):
                nonlocal ev_omm2_last
                for hf in range(NHF):
                    for j in range(g * GT, (g + 1) * GT):
                        oi = len(ev_oev2)

                        def f_omm(e, j=j, hf=hf, oi=oi):
                            return e.matmul(
                                pos[oi % 2].ap()[:, 0:HFW],
                                ctx_pair[:, j * 128:(j + 1) * 128],
                                ow_sb[:, hf * HFW:(hf + 1) * HFW],
                                start=True, stop=True)
                        ev_omm2 = ph.op("tensor", f_omm,
                                        waits=[ev_mv,
                                               ev_ow_next,
                                               ev_oev2[oi - 2] if oi >= 2 else None])
                        ev_omm2_last = ev_omm2
                        ev_oev2.append(ph.op(
                            "vector", lambda e, j=j, hf=hf, oi=oi: e.tensor_tensor(
                                x_res.ap()[:, j * D + hf * HFW: j * D + (hf + 1) * HFW],
                                x_res.ap()[:, j * D + hf * HFW: j * D + (hf + 1) * HFW],
                                pos[oi % 2].ap()[:, 0:HFW], op=ALU.add),
                            waits=[ev_omm2]))

            for g in range(NQG):
                nki = GKT * (g + 1)
                ev_exp_k = [[None] * nki, [None] * nki]
                ev_ms_k = [[None] * nki, [None] * nki]

                def emit_pv(ki, phase=ph, g=g, nki=nki, ev_exp_k=ev_exp_k,
                            ev_ms_k=ev_ms_k):
                    for hp in range(2):
                        sl = ki % 3

                        def f_pv(e, hp=hp, sl=sl, ki=ki, g=g, nki=nki):
                            return e.matmul(
                                pcs[hp].ap()[0:65, :],
                                v_pair[:, ki * VT + hp * (HD + 1): ki * VT + (hp + 1) * (HD + 1)],
                                pT_sb[:, (2 * hp + sl if False else hp * 3 + sl) * QG:
                                      (hp * 3 + sl + 1) * QG],
                                start=(ki == 0), stop=(ki == nki - 1))
                        ev = phase.op(
                            "tensor", f_pv,
                            waits=[ev_exp_k[hp][ki], ev_ms_k[hp][ki],
                                   ev_cx[hp] if ki == 0 else None])
                        lastpv[hp][sl] = ev

                for ki in range(nki):
                    diag = ki >= GKT * g
                    moff = (ki - GKT * g) * 128 if diag else 0
                    nq = QG - moff
                    for hp in range(2):
                        r0 = hp * 64
                        psl = (2 * ki + hp) % 3
                        sl = ki % 3

                        def f_mmT(e, hp=hp, r0=r0, psl=psl, ki=ki, g=g, moff=moff, nq=nq):
                            return e.matmul(
                                pTs[psl].ap()[:, 0:nq],
                                k_pair[r0:r0 + 64, ki * 128:(ki + 1) * 128],
                                q_pair[r0:r0 + 64, g * QG + moff: (g + 1) * QG],
                                start=True, stop=True, tile_position=(r0, 0))
                        ev_mmT = ph.op("tensor", f_mmT,
                                       waits=[lastexp[psl]])
                        ev_msk = None
                        if diag:
                            ev_msk = ph.op("vector", lambda e, psl=psl:
                                           e.tensor_tensor(
                                               pTs[psl].ap()[:, 0:128],
                                               pTs[psl].ap()[:, 0:128],
                                               masknegT.ap(), op=ALU.add),
                                           waits=[ev_mmT])
                        ev_ms = None
                        if moff:
                            ev_ms = ph.op("vector", lambda e, hp=hp, sl=sl, moff=moff:
                                          e.memset(pT_sb[:, (hp * 3 + sl) * QG:
                                                         (hp * 3 + sl) * QG + moff], 0.0),
                                          waits=[lastpv[hp][sl]])

                        def f_exp(e, hp=hp, psl=psl, sl=sl, moff=moff, nq=nq):
                            return e.activation(
                                pT_sb[:, (hp * 3 + sl) * QG + moff:(hp * 3 + sl + 1) * QG],
                                pTs[psl].ap()[:, 0:nq], AF.Exp,
                                bias=sb["zero_b"].ap(), scale=c.scale_b)
                        ev_exp_k[hp][ki] = ph.op(
                            "scalar", f_exp,
                            waits=[ev_msk if diag else ev_mmT,
                                   lastpv[hp][sl]])
                        lastexp[psl] = ev_exp_k[hp][ki]
                        ev_ms_k[hp][ki] = ev_ms
                    if ki == 1 and pend_out:
                        emit_outproj(*pend_out.pop(0))
                    if ki >= 2:
                        emit_pv(ki - 2)
                for ki in (nki - 2, nki - 1):
                    if ki >= 0:
                        emit_pv(ki)
                for hp in range(2):
                    def f_dcp(e, hp=hp):
                        return e.activation(
                            rden_sb[64:65, :], pcs[hp].ap()[64:65, :], AF.Ln,
                            bias=sb["zero_b"].ap()[64:65, :])
                    ev_dc = ph.op("scalar", f_dcp,
                                  waits=[lastpv[hp][(nki - 1) % 3], ev_rbm_prev])
                    ev_rbm = ph.op("tensor", lambda e: e.matmul(
                        ps_rb.ap()[0:64, :], onesb.ap()[64:65, 0:64],
                        rden_sb[64:65, :], start=True, stop=True),
                        waits=[ev_dc, ev_rbc_prev])
                    ev_rbm_prev = ev_rbm
                    ev_rbc = ph.op("scalar", lambda e: e.activation(
                        rb_sb[0:64, :], ps_rb.ap()[0:64, :], AF.Exp,
                        bias=sb["zero_b"].ap()[0:64, :], scale=-1.0),
                        waits=[ev_rbm, ev_cx_last])
                    ev_rbc_prev = ev_rbc

                    def f_cx(e, hp=hp, g=g):
                        return e.tensor_tensor(
                            ctx_h[hp][0:64, g * QG:(g + 1) * QG],
                            pcs[hp].ap()[0:64, :],
                            rb_sb[0:64, :], op=ALU.mult)
                    ev_cx[hp] = ph.op("vector", f_cx, waits=[ev_rbc])
                    ev_cx_last = ev_cx[hp]
                ev_ctxmv = ph.op("sync",
                                 lambda e, g=g: e.dma_start(
                                     ctx_pair[64:128, g * QG:(g + 1) * QG],
                                     ctx_b[0:64, g * QG:(g + 1) * QG]),
                                 dma=1, channel="ctxmv",
                                 waits=[ev_cx[1], ev_ctxmv])
                pend_out.append((g, ev_ctxmv))
            while pend_out:
                emit_outproj(*pend_out.pop(0))
            if p + 1 < H // 2:
                ev_w_next = ph.op("sync", lambda e, p=p: f_w_pair(e, p + 1),
                                  dma=3, channel="abw",
                                  waits=[ev_projmm_last])
                ev_ow_next = ph.op(
                    "sync", lambda e, p=p: e.dma_start(
                        ow_sb, dram["woutT"][(p + 1) * 128:(p + 2) * 128, :]),
                    dma=1, channel="abo", waits=[ev_omm2_last])
            ph.emit()


F = np.float32


def host_arrays(cfg, w):
    D, NCH, H = cfg.D, cfg.NCH, cfg.H
    D2 = D // 2
    NP = H // 2
    out = {}
    out["wq_cg"] = np.ascontiguousarray(
        np.stack([w["wq"][:, cg * 128:(cg + 1) * 128] for cg in range(NCH)]))
    out["wk_half"] = np.ascontiguousarray(
        np.stack([w["wk"][:, h * D2:(h + 1) * D2] for h in range(2)]))
    out["wv_half"] = np.ascontiguousarray(
        np.stack([w["wv"][:, h * D2:(h + 1) * D2] for h in range(2)]))
    out["wo"] = np.ascontiguousarray(w["wo"])
    out["bq_pp"] = np.ascontiguousarray(w["bq"].reshape(NCH, 128).T)
    out["bk_bc"] = np.ascontiguousarray(np.tile(w["bk"][None, :], (128, 1)))
    out["bv_bc"] = np.ascontiguousarray(np.tile(w["bv"][None, :], (128, 1)))
    out["bo_bc"] = np.ascontiguousarray(np.tile(w["bo"][None, :], (128, 1)))
    wqkvT = w["wqkvT"]
    out["wqkv_p"] = np.ascontiguousarray(np.stack([
        np.stack([wqkvT[:, which * D + p * 128: which * D + (p + 1) * 128]
                  for which in range(3)])
        for p in range(NP)]))
    out["woutT"] = np.ascontiguousarray(w["woutT"])
    out["wrT"] = np.ascontiguousarray(w["wrT"])
    out["wkT2"] = np.ascontiguousarray(w["wkT2"])
    out["wvT2"] = np.ascontiguousarray(w["wvT2"])
    out["mur_pp"] = np.ascontiguousarray(w["mu_r"].reshape(NCH, 128).T)
    out["muk_pp"] = np.ascontiguousarray(w["mu_k"].reshape(NCH, 128).T)
    out["ident"] = np.eye(128, dtype=F)
    mn = np.zeros((128, 128), F)
    iu = np.triu_indices(128, 1)
    mn[iu] = np.float32(-1.0e30)
    out["maskneg"] = mn
    out["masknegT"] = np.ascontiguousarray(mn.T)
    rs = np.zeros((2, 128), F)
    rs[0, 0:64] = 1.0
    rs[1, 64:128] = 1.0
    out["rowsel"] = rs
    for k in out:
        assert out[k].dtype == np.float32, k
    return out


def rand_logical(cfg, seed=0):
    rng = np.random.default_rng(seed)
    D = cfg.D

    def r(*s, scale=1.0):
        return (rng.standard_normal(s) * scale).astype(F)
    w = dict(
        wq=r(D, D, scale=D ** -0.5), wk=r(D, D, scale=D ** -0.5),
        wv=r(D, D, scale=D ** -0.5), wo=r(D, D, scale=D ** -0.5),
        bq=r(D, scale=1e-3), bk=r(D, scale=1e-3), bv=r(D, scale=1e-3),
        bo=r(D, scale=1e-3),
        wqkvT=r(D, 3 * D, scale=(2 * D) ** -0.5),
        woutT=r(D, D, scale=D ** -0.5),
        wrT=r(D, D, scale=D ** -0.5), wkT2=r(D, D, scale=D ** -0.5),
        wvT2=r(D, D, scale=D ** -0.5),
        mu_r=rng.uniform(0, 1, D).astype(F), mu_k=rng.uniform(0, 1, D).astype(F),
    )
    return w


from concourse.bass_utils import run_bass_kernel_spmd

N_CORES = 8
FULL = dict(seqs=2, L=2048, D=1024, H=16, HD=64, upto=4)


def _logical_weights(inputs):
    D = 1024
    FF = np.float32

    def a(k):
        return np.asarray(inputs[k], FF)
    return dict(
        wq=np.ascontiguousarray(a("L0_attn_a_Wq").reshape(D, D)),
        wk=np.ascontiguousarray(a("L0_attn_a_Wk").reshape(D, D)),
        wv=np.ascontiguousarray(a("L0_attn_a_Wv").reshape(D, D)),
        wo=np.ascontiguousarray(a("L0_attn_a_Wo").reshape(D, D)),
        bq=a("L0_attn_a_bq").reshape(-1),
        bk=a("L0_attn_a_bk").reshape(-1),
        bv=a("L0_attn_a_bv").reshape(-1),
        bo=a("L0_attn_a_bo").reshape(-1),
        wqkvT=np.ascontiguousarray(a("L0_attn_b_w_qkv").T),
        woutT=np.ascontiguousarray(a("L0_attn_b_w_out").T),
        wrT=np.ascontiguousarray(a("L0_ffn_W_r").T),
        wkT2=np.ascontiguousarray(a("L0_ffn_W_k").T),
        wvT2=np.ascontiguousarray(a("L0_ffn_W_v").T),
        mu_r=a("L0_ffn_mu_r"), mu_k=a("L0_ffn_mu_k"),
    )


def kernel(**inputs):
    cfg = Cfg(**FULL)
    w = _logical_weights(inputs)
    warrs = host_arrays(cfg, w)
    x = np.ascontiguousarray(np.asarray(inputs["L0_attn_a_x"], np.float32))
    B = x.shape[0]
    per = B // N_CORES

    nc = build(cfg)
    in_maps = []
    for core in range(N_CORES):
        m = dict(warrs)
        m["xin"] = np.ascontiguousarray(x[core * per:(core + 1) * per])
        in_maps.append(m)
    res = run_bass_kernel_spmd(nc, in_maps, core_ids=list(range(N_CORES)))
    out = np.concatenate([np.asarray(r["yout"]) for r in res.results], axis=0)
    return out.astype(np.float32, copy=False)
